# version 03
# baseline (speedup 1.0000x reference)
import itertools
import numpy as np
import ml_dtypes
from contextlib import ExitStack

import concourse.bass as bass
import concourse.mybir as mybir
from concourse.bass_utils import run_bass_kernel_spmd

F32 = mybir.dt.float32
BF16 = mybir.dt.bfloat16
I32 = mybir.dt.int32
AF = mybir.ActivationFunctionType
ALU = mybir.AluOpType
AX = mybir.AxisListType

D = 4096
DC = 32
TOK = 1024
TC = 8
HEADS = 16
ROT = 64
NEG = -1.0e9
EPS = 1e-5
SHUF = [i + 1 - 2 * (i % 2) for i in range(32)]

_EPOCH = itertools.count(1)
DEBUG = False


def _rope_tables(p):
    inv = (1.0 / 10000.0 ** (np.arange(0, ROT, 2, dtype=np.float32) / ROT)).astype(np.float32)
    tg = np.concatenate([np.arange((2 * j + p) * 128, (2 * j + p + 1) * 128) for j in range(TC)])
    ang = tg[:, None].astype(np.float32) * inv[None, :]
    cos_fm = np.ascontiguousarray(np.cos(ang).T[np.repeat(np.arange(32), 2)]).astype(np.float32)
    sin_fm = np.ascontiguousarray(np.sin(ang).T[np.repeat(np.arange(32), 2)]).astype(np.float32)
    return cos_fm, sin_fm


def _mask_tables(p):
    tri = np.where(np.arange(128)[:, None] >= np.arange(128)[None, :], 0.0, NEG).astype(np.float32)
    m = np.zeros((TC, 128, 256), np.float32)
    for j in range(TC):
        if p == 0:
            m[j, :, 0:128] = tri
            m[j, :, 128:256] = NEG
        else:
            m[j, :, 128:256] = tri
    return m


class Sem:
    def __init__(self, nc, ctx, name):
        self.h = ctx.enter_context(nc.semaphore(name))
        self.n = 0

    def inc(self, instr, k=1):
        instr.then_inc(self.h, k)
        self.n += k
        return self.n

    def dinc(self, instr):
        return self.inc(instr, 16)


class Alloc:

    def __init__(self, nc):
        self.nc = nc
        self.stacks = {"left": [], "right": []}

    def get(self, name, shape, dtype, side="left"):
        g = self.nc.sbuf_tensor(name, shape, dtype, side=side)
        h = g.__enter__()
        self.stacks[side].append((name, g))
        return h

    def free(self, name, side="left"):
        nm, g = self.stacks[side].pop()
        assert nm == name, (nm, name)
        g.__exit__(None, None, None)


def build_program():
    nc = bass.Bass("TRN2", target_bir_lowering=False)

    x_in = nc.dram_tensor("x_own", [TOK, D], F32, kind="ExternalInput").ap()
    w_in = {m: nc.dram_tensor(m, [D, D], F32, kind="ExternalInput").ap()
            for m in ("wq", "wk", "wv", "wo")}
    a2t_in = nc.dram_tensor("a2t", [D, 128], F32, kind="ExternalInput").ap()
    b2_in = nc.dram_tensor("b2", [128, D], F32, kind="ExternalInput").ap()
    wr_in = nc.dram_tensor("wr8", [8, D], F32, kind="ExternalInput").ap()
    br_in = nc.dram_tensor("br8", [1, 8], F32, kind="ExternalInput").ap()
    cosfm_in = nc.dram_tensor("cosfm_i", [ROT, TOK], F32, kind="ExternalInput").ap()
    sinfm_in = nc.dram_tensor("sinfm_i", [ROT, TOK], F32, kind="ExternalInput").ap()
    maskq_in = nc.dram_tensor("maskq", [TC, 128, 256], F32, kind="ExternalInput").ap()
    magic_in = nc.dram_tensor("magic", [1, 4], I32, kind="ExternalInput").ap()

    out_hbm = nc.dram_tensor("out", [TOK, D], F32, kind="ExternalOutput").ap()
    dbg = {}
    if DEBUG:
        dbg["k"] = nc.dram_tensor("dbg_k", [D, TOK], BF16, kind="ExternalOutput").ap()
        dbg["v"] = nc.dram_tensor("dbg_v", [TOK, D], BF16, kind="ExternalOutput").ap()
        dbg["q"] = nc.dram_tensor("dbg_q", [D, TOK], BF16, kind="ExternalOutput").ap()
        dbg["at"] = nc.dram_tensor("dbg_at", [D, TOK], BF16, kind="ExternalOutput").ap()
        dbg["y"] = nc.dram_tensor("dbg_y", [TOK, D], F32, kind="ExternalOutput").ap()

    KSH = nc.dram_tensor("KSH", [2 * D, TOK], BF16, kind="Internal", addr_space="Shared").ap()
    VSH = nc.dram_tensor("VSH", [2 * TOK, D], BF16, kind="Internal", addr_space="Shared").ap()
    SHF = nc.dram_tensor("SHF", [1, 8], I32, kind="Internal", addr_space="Shared").ap()
    kloc = nc.dram_tensor("kloc", [D, TOK], BF16, kind="Internal").ap()
    qloc = nc.dram_tensor("qloc", [D, TOK], BF16, kind="Internal").ap()
    vloc = nc.dram_tensor("vloc", [TOK, D], BF16, kind="Internal").ap()
    attnT_d = nc.dram_tensor("attnT_d", [D, TOK], BF16, kind="Internal").ap()
    y_d = nc.dram_tensor("y_d", [TOK, D], F32, kind="Internal").ap()

    i128c = nc.inline_tensor(np.eye(128, dtype=ml_dtypes.bfloat16), "i128c").ap()
    i8cc = nc.inline_tensor(np.eye(8, dtype=ml_dtypes.bfloat16), "i8cc").ap()
    ones_c = nc.inline_tensor(np.ones((1, 128), dtype=ml_dtypes.bfloat16), "ones_cc").ap()
    e8_np = np.zeros((8, 128), dtype=ml_dtypes.bfloat16)
    for h in range(8):
        e8_np[h, h * 16:(h + 1) * 16] = 1.0
    e8c = nc.inline_tensor(e8_np, "e8c").ap()

    ctx = ExitStack()
    with ctx:
        en = ctx.enter_context
        al = Alloc(nc)

        ident = en(nc.sbuf_tensor("ident", [128, 128], BF16))
        i8b = en(nc.sbuf_tensor("i8b", [8, 8], BF16))
        e8b = en(nc.sbuf_tensor("e8b", [8, 128], BF16))
        ones1 = en(nc.sbuf_tensor("ones1", [1, 128], BF16))
        maskb = en(nc.sbuf_tensor("maskb", [128, TC * 256], BF16))
        cosfm = en(nc.sbuf_tensor("cosfm", [ROT, TOK], F32))
        sinfm = en(nc.sbuf_tensor("sinfm", [ROT, TOK], F32))
        wrtb = en(nc.sbuf_tensor("wrtb", [128, DC * 8], BF16))
        brb = en(nc.sbuf_tensor("brb", [1, 8], BF16))
        mg = en(nc.sbuf_tensor("mg", [1, 4], I32))
        fl = en(nc.sbuf_tensor("fl", [1, 4], I32))
        st = en(nc.sbuf_tensor("st", [128, 32], F32))

        sXQ = Sem(nc, ctx, "sXQ")
        sWG = Sem(nc, ctx, "sWG")
        sSTG = Sem(nc, ctx, "sSTG")
        sOUT = Sem(nc, ctx, "sOUT")
        sTE = Sem(nc, ctx, "sTE")
        sV = Sem(nc, ctx, "sV")
        sSC = Sem(nc, ctx, "sSC")
        sEVV = Sem(nc, ctx, "sEVV")
        sEVS = Sem(nc, ctx, "sEVS")
        sG = Sem(nc, ctx, "sG")
        sPOLL = Sem(nc, ctx, "sPOLL")
        sPOLL2 = Sem(nc, ctx, "sPOLL2")

        blk1 = nc.Block("b1")
        blk1.__enter__()
        TE, VE, SC, GE, SY = blk1.tensor, blk1.vector, blk1.scalar, blk1.gpsimd, blk1.sync

        def p0_g(g):
            sWG.dinc(g.dma_start(out=ident[:], in_=i128c[:, :]))
            sWG.dinc(g.dma_start(out=i8b[:], in_=i8cc[:, :]))
            sWG.dinc(g.dma_start(out=e8b[:], in_=e8c[:, :]))
            sWG.dinc(g.dma_start(out=ones1[:], in_=ones_c[:, :]))
            sWG.dinc(g.dma_start(out=maskb[:, :].rearrange("p (j k) -> p j k", j=TC), in_=maskq_in.rearrange("j p k -> p j k")[:, :, :]))
            sWG.dinc(g.dma_start(out=brb[:], in_=br_in[:, :]))
        GE(p0_g)
        CONSTS_G = sWG.n

        def p0_v(v):
            v.memset(st[:, 30:31], EPS)
            v.memset(st[:, 31:32], 0.0)
        VE(p0_v)

        def p0_s(sync):
            sXQ.dinc(sync.dma_start(out=mg[:], in_=magic_in[:, :]))
            sXQ.dinc(sync.dma_start(out=cosfm[:], in_=cosfm_in[:, :]))
            sXQ.dinc(sync.dma_start(out=sinfm[:], in_=sinfm_in[:, :]))
        SY(p0_s)

        WRT_EV_END = sEVV.n

        ln0T = al.get("ln0T", [128, DC * TOK], BF16)

        def ln0T_c(c):
            return ln0T[:, c * TOK:(c + 1) * TOK]

        xbuf = al.get("xbuf", [128, 2 * D], F32)
        ln0c = al.get("ln0c", [128, 2 * D], BF16)
        ps_tp_g = [nc.psum_tensor(f"ps_tp{i}", [128, 512], F32) for i in range(2)]
        ps_tp = [g.__enter__() for g in ps_tp_g]

        xload, ln_done, tp_ev = [], [], []
        for j in range(TC):
            xs = xbuf[:, (j % 2) * D:(j % 2 + 1) * D]
            lns = ln0c[:, (j % 2) * D:(j % 2 + 1) * D]

            def p1_sy(sync, j=j, xs=xs):
                if j >= 2:
                    sync.wait_ge(sV.h, ln_done[j - 2])
                sXQ.dinc(sync.dma_start(out=xs, in_=x_in[j * 128:(j + 1) * 128, :]))
            SY(p1_sy)
            xload.append(sXQ.n)

            def p1_v1(v, j=j, xs=xs, lns=lns):
                v.wait_ge(sXQ.h, xload[j])
                v.tensor_scalar(lns, xs, 1.0, 0.0, ALU.mult, ALU.add, accum_out=st[:, 0:1])
                kacc = sV.inc(v.scalar_tensor_tensor(lns, xs, 1.0, xs, ALU.mult,
                                                     ALU.mult, accum_out=st[:, 1:2]))
                v.wait_ge(sV.h, kacc)
                v.tensor_scalar_mul(st[:, 2:3], st[:, 0:1], 1.0 / D)
                v.tensor_scalar_mul(st[:, 5:6], st[:, 1:2], 1.0 / D)
                kmu = sV.inc(v.tensor_mul(st[:, 6:7], st[:, 2:3], st[:, 2:3]))
                v.wait_ge(sV.h, kmu)
                sV.inc(v.tensor_sub(st[:, 3:4], st[:, 5:6], st[:, 6:7]))
            VE(p1_v1)
            var_ready = sV.n

            def p1_sc1(sc, var_ready=var_ready):
                sc.wait_ge(sV.h, var_ready)
                k = sSC.inc(sc.activation(st[:, 7:8], st[:, 3:4], AF.Ln, bias=st[:, 30:31]))
                sc.wait_ge(sSC.h, k)
                sSC.inc(sc.activation(st[:, 4:5], st[:, 7:8], AF.Exp, bias=st[:, 31:32], scale=-0.5))
            SC(p1_sc1)
            rstd_ready = sSC.n

            def p1_v2(v, xs=xs, lns=lns, rstd_ready=rstd_ready):
                v.wait_ge(sSC.h, rstd_ready)
                sV.inc(v.tensor_scalar(lns, xs, st[:, 2:3], st[:, 4:5],
                                       ALU.subtract, ALU.mult))
            VE(p1_v2)
            ln_done.append(sV.n)

            for gi in range(8):
                gg = 8 * j + gi

                def p1_te(te, j=j, gi=gi, gg=gg, lns=lns):
                    if gi == 0:
                        te.wait_ge(sV.h, ln_done[j])
                    if gg >= 2:
                        te.wait_ge(sEVS.h, tp_ev[gg - 2])
                    bank = ps_tp[gg % 2]
                    ins = None
                    for u in range(4):
                        c = 4 * gi + u
                        ins = te.matmul(bank[:, u * 128:(u + 1) * 128],
                                        lns[:, c * 128:(c + 1) * 128], ident[:],
                                        start=True, stop=True)
                    sTE.inc(ins)
                TE(p1_te)
                tpd = sTE.n

                def p1_sc2(sc, j=j, gi=gi, gg=gg, tpd=tpd):
                    sc.wait_ge(sTE.h, tpd)
                    bank = ps_tp[gg % 2]
                    dst = ln0T[:, :].rearrange("p (c t) -> p c t", c=DC)[
                        :, 4 * gi:4 * gi + 4, j * 128:(j + 1) * 128]
                    src = bank[:, :].rearrange("p (a b) -> p a b", a=4)
                    sEVS.inc(sc.copy(dst, src))
                SC(p1_sc2)
                tp_ev.append(sEVS.n)

        P1_V_END = sV.n
        P1_EVS_END = sEVS.n
        for g in reversed(ps_tp_g):
            g.__exit__(None, None, None)
        al.free("ln0c")
        al.free("xbuf")

        ps4_g = [nc.psum_tensor(f"ps4_{i}", [128, 512], F32) for i in range(4)]
        ps4 = [g.__enter__() for g in ps4_g]

        def emit_fm(kind, w_dram, dest_dram, gate_evv, gate_v=None, gate_evs=None,
                    gate_te=None, gate_stg=None):
            wcol = al.get("wcol" + kind, [128, 2 * D], BF16)
            kqs = al.get("kqs" + kind, [128, 2 * TOK], BF16)
            rsc = al.get("rsc" + kind, [64, 2 * 512], F32)
            wload, mmdone, evdone, stgdone = [], [], [], []
            for oc in range(DC):
                def fm_g(g, oc=oc):
                    if oc == 0:
                        g.wait_ge(sEVV.h, gate_evv)
                        if gate_v is not None:
                            g.wait_ge(sV.h, gate_v)
                        if gate_evs is not None:
                            g.wait_ge(sEVS.h, gate_evs)
                        if gate_te is not None:
                            g.wait_ge(sTE.h, gate_te)
                        if gate_stg is not None:
                            g.wait_ge(sSTG.h, gate_stg)
                    if oc >= 2:
                        g.wait_ge(sTE.h, mmdone[oc - 2])
                    src = w_dram[:, oc * 128:(oc + 1) * 128].rearrange(
                        "(c p) o -> p c o", p=128)
                    dst = wcol[:, (oc % 2) * D:(oc % 2 + 1) * D].rearrange(
                        "p (c o) -> p c o", c=DC)
                    sWG.dinc(g.dma_start(out=dst, in_=src[:, :, :]))
                GE(fm_g)
                wload.append(sWG.n)

                def fm_te(te, oc=oc):
                    if oc == 0:
                        te.wait_ge(sEVS.h, P1_EVS_END)
                        te.wait_ge(sEVV.h, gate_evv)
                    te.wait_ge(sWG.h, wload[oc])
                    if oc >= 2:
                        te.wait_ge(sEVV.h, evdone[oc - 2])
                    ins = None
                    for tg in range(2):
                        bank = ps4[2 * (oc % 2) + tg]
                        for c in range(DC):
                            lhs = wcol[:, (oc % 2) * D + c * 128:(oc % 2) * D + (c + 1) * 128]
                            ins = te.matmul(bank[:, :], lhs,
                                            ln0T_c(c)[:, tg * 512:(tg + 1) * 512],
                                            start=(c == 0), stop=(c == DC - 1))
                    sTE.inc(ins)
                TE(fm_te)
                mmdone.append(sTE.n)

                def fm_v(v, oc=oc):
                    v.wait_ge(sTE.h, mmdone[oc])
                    if oc >= 2:
                        v.wait_ge(sSTG.h, stgdone[oc - 2])
                    dstc = kqs[:, (oc % 2) * TOK:(oc % 2 + 1) * TOK]
                    ins = None
                    for tg in range(2):
                        bank = ps4[2 * (oc % 2) + tg]
                        dst = dstc[:, tg * 512:(tg + 1) * 512]
                        if oc % 2 == 0:
                            qs = rsc[:, 0:512]
                            t1 = rsc[:, 512:1024]
                            v.stream_shuffle(qs, bank[0:64, :], mask=SHUF)
                            v.tensor_tensor(t1, qs, sinfm[:, tg * 512:(tg + 1) * 512], ALU.mult)
                            v.tensor_tensor(qs, bank[0:64, :], cosfm[:, tg * 512:(tg + 1) * 512], ALU.mult)
                            v.tensor_add(dst[0:64, :], qs, t1)
                            ins = v.tensor_copy(dst[64:128, :], bank[64:128, :])
                        else:
                            ins = v.tensor_copy(dst, bank[:, :])
                    sEVV.inc(ins)
                VE(fm_v)
                evdone.append(sEVV.n)

                def fm_sy(sync, oc=oc):
                    sync.wait_ge(sEVV.h, evdone[oc])
                    sSTG.dinc(sync.dma_start(
                        out=dest_dram[oc * 128:(oc + 1) * 128, :],
                        in_=kqs[:, (oc % 2) * TOK:(oc % 2 + 1) * TOK]))
                SY(fm_sy)
                stgdone.append(sSTG.n)
            al.free("rsc" + kind)
            al.free("kqs" + kind)
            al.free("wcol" + kind)
            return mmdone, evdone, stgdone

        k_mm, k_ev, k_stg = emit_fm('K', w_in["wk"], kloc, WRT_EV_END,
                                    gate_v=P1_V_END, gate_evs=P1_EVS_END)

        wvb = al.get("wvb", [128, 2 * DC * 512], BF16)
        vsb = al.get("vsb", [128, 2 * 512], BF16)
        v_wload, v_mm, v_ev, v_stg = [], {}, {}, {}
        for ob in range(8):
            def v_g(g, ob=ob):
                if ob == 0:
                    g.wait_ge(sEVS.h, P1_EVS_END)
                    g.wait_ge(sTE.h, k_mm[-1])
                    g.wait_ge(sEVV.h, k_ev[-1])
                    g.wait_ge(sSTG.h, k_stg[-1])
                if ob >= 2:
                    g.wait_ge(sTE.h, v_mm[(ob - 2, TC - 1)])
                src = w_in["wv"][:, ob * 512:(ob + 1) * 512].rearrange(
                    "(c p) o -> p c o", p=128)
                dst = wvb[:, (ob % 2) * DC * 512:(ob % 2 + 1) * DC * 512].rearrange(
                    "p (c o) -> p c o", c=DC)
                sWG.dinc(g.dma_start(out=dst, in_=src[:, :, :]))
            GE(v_g)
            v_wload.append(sWG.n)

            for tk in range(TC):
                t4 = tk % 4

                def v_te(te, ob=ob, tk=tk, t4=t4):
                    if ob == 0 and tk == 0:
                        te.wait_ge(sEVV.h, k_ev[-1])
                    te.wait_ge(sWG.h, v_wload[ob])
                    prev = (ob, tk - 4) if tk >= 4 else ((ob - 1, tk + 4) if ob >= 1 else None)
                    if prev is not None:
                        te.wait_ge(sEVV.h, v_ev[prev])
                    bank = ps4[t4]
                    ins = None
                    for c in range(DC):
                        lhs = ln0T_c(c)[:, tk * 128:(tk + 1) * 128]
                        rhs = wvb[:, (ob % 2) * DC * 512 + c * 512:
                                  (ob % 2) * DC * 512 + (c + 1) * 512]
                        ins = te.matmul(bank[:, :], lhs, rhs,
                                        start=(c == 0), stop=(c == DC - 1))
                    sTE.inc(ins)
                TE(v_te)
                v_mm[(ob, tk)] = sTE.n

                def v_v(v, ob=ob, tk=tk, t4=t4):
                    v.wait_ge(sTE.h, v_mm[(ob, tk)])
                    nsl = (ob * TC + tk) % 2
                    if ob * TC + tk >= 2:
                        v.wait_ge(sSTG.h, v_stg[ob * TC + tk - 2])
                    sEVV.inc(v.tensor_copy(vsb[:, nsl * 512:(nsl + 1) * 512],
                                           ps4[t4][:, :]))
                VE(v_v)
                v_ev[(ob, tk)] = sEVV.n

                def v_sy(sync, ob=ob, tk=tk):
                    sync.wait_ge(sEVV.h, v_ev[(ob, tk)])
                    nsl = (ob * TC + tk) % 2
                    sSTG.dinc(sync.dma_start(
                        out=vloc[tk * 128:(tk + 1) * 128, ob * 512:(ob + 1) * 512],
                        in_=vsb[:, nsl * 512:(nsl + 1) * 512]))
                SY(v_sy)
                v_stg[ob * TC + tk] = sSTG.n
        V_EV_END = sEVV.n
        al.free("vsb")
        al.free("wvb")

        q_mm, q_ev, q_stg = emit_fm('Q', w_in["wq"], qloc, V_EV_END,
                                    gate_te=v_mm[(7, TC - 1)],
                                    gate_stg=v_stg[8 * TC - 1])
        Q_TE_END = sTE.n
        Q_EV_END = sEVV.n
        STG_PRE_ARMS = sSTG.n

        for g in reversed(ps4_g):
            g.__exit__(None, None, None)
        al.free("ln0T")

        blk1.__exit__(None, None, None)

        for arm in nc.switch_core_id([nc.sync], n=128):
            slot = int(arm) % 2
            nc.sync.wait_ge(sSTG.h, STG_PRE_ARMS)
            nc.sync.wait_ge(sXQ.h, 16)
            nc.sync.dma_start(out=KSH[slot * D:(slot + 1) * D, :],
                              in_=kloc[:, :]).then_inc(sSTG.h, 16)
            nc.sync.dma_start(out=VSH[slot * TOK:(slot + 1) * TOK, :],
                              in_=vloc[:, :]).then_inc(sSTG.h, 16)
            nc.sync.wait_ge(sSTG.h, STG_PRE_ARMS + 32)
            nc.sync.dma_start(out=SHF[0:1, slot:slot + 1],
                              in_=mg[0:1, 0:1]).then_inc(sSTG.h, 16)
            nc.sync.wait_ge(sSTG.h, STG_PRE_ARMS + 48)
        sSTG.n = STG_PRE_ARMS + 48

        blk2 = nc.Block("b2")
        blk2.__enter__()
        TE, VE, SC, GE, SY = blk2.tensor, blk2.vector, blk2.scalar, blk2.gpsimd, blk2.sync

        def emit_poll(sync, col0, mgcol, sem):
            with sync.register() as r_exp, sync.register() as r_f0, \
                    sync.register() as r_f1, sync.register() as r_c, \
                    sync.register() as r_m:
                sync.load(r_m, mg[0:1, mgcol:mgcol + 1])
                sync.reg_mov(r_exp, 0)

                def cond():
                    sync.dma_start(out=fl[0:1, 0:2],
                                   in_=SHF[0:1, col0:col0 + 2]).then_inc(sem.h, 16)
                    sync.reg_add(r_exp, r_exp, 16)
                    sync.wait_ge(sem.h, r_exp)
                    sync.load(r_f0, fl[0:1, 0:1])
                    sync.load(r_f1, fl[0:1, 1:2])
                    sync.reg_alu(r_c, r_f0, r_m, ALU.is_equal)
                    sync.reg_alu(r_f1, r_f1, r_m, ALU.is_equal)
                    sync.reg_alu(r_c, r_c, r_f1, ALU.bitwise_and)
                    sync.reg_alu(r_c, r_c, 1, ALU.is_lt)
                    return r_c

                with sync.While(cond):
                    sync.nop()

        def p3_sy(sync):
            emit_poll(sync, 0, 0, sPOLL)
            sOUT.inc(sync.nop())
        SY(p3_sy)
        POLL_DONE = sOUT.n

        kT = al.get("kT", [128, 2 * 4096], BF16)
        qTh = al.get("qTh", [128, 2 * 2048], BF16)
        vhb = al.get("vhb", [128, 2 * 4096], BF16)
        Psb = al.get("Psb", [128, 2048], BF16)
        PT = al.get("PT", [128, 16 * 1024], BF16)
        diag = al.get("diag", [128, 128], BF16)
        atb = al.get("atb", [128, 2 * 512], BF16)
        ps_s_g = nc.psum_tensor("ps_s", [128, 2048], F32)
        ps_s = ps_s_g.__enter__()
        ps_pt_g = [nc.psum_tensor(f"ps_pt{i}", [128, 512], F32) for i in range(2)]
        ps_pt = [g.__enter__() for g in ps_pt_g]
        ps_av_g = [nc.psum_tensor(f"ps_av{i}", [128, 512], F32) for i in range(2)]
        ps_av = [g.__enter__() for g in ps_av_g]

        kv_load, head_te_end = [], []
        exp_done, avev_done, atw_done = {}, {}, {}
        ptev_list = []
        ptg = 0

        for h in range(HEADS):
            hb = h % 2

            def p4_g(g, h=h, hb=hb):
                if h == 0:
                    g.wait_ge(sOUT.h, POLL_DONE)
                    g.wait_ge(sTE.h, Q_TE_END)
                    g.wait_ge(sEVV.h, Q_EV_END)
                    g.wait_ge(sSTG.h, STG_PRE_ARMS)
                if h >= 2:
                    g.wait_ge(sTE.h, head_te_end[h - 2])
                for dh in range(2):
                    src = qloc[(2 * h + dh) * 128:(2 * h + dh + 1) * 128, :]
                    dst = qTh[:, hb * 2048 + dh * 1024:hb * 2048 + (dh + 1) * 1024]
                    sWG.dinc(g.dma_start(out=dst, in_=src))
                for dh in range(2):
                    for s in range(2):
                        src = KSH[s * D + (2 * h + dh) * 128:
                                  s * D + (2 * h + dh + 1) * 128, :]
                        dst = kT[:, hb * 4096 + dh * 2048 + s * 1024:
                                 hb * 4096 + dh * 2048 + (s + 1) * 1024]
                        sWG.dinc(g.dma_start(out=dst, in_=src))
                for s in range(2):
                    src = VSH[s * TOK:(s + 1) * TOK, h * 256:(h + 1) * 256] \
                        .rearrange("(u p) d -> p u d", p=128)
                    dst = vhb[:, hb * 4096 + s * 2048:hb * 4096 + (s + 1) * 2048] \
                        .rearrange("p (u d) -> p u d", u=8)
                    sWG.dinc(g.dma_start(out=dst, in_=src[:, :, :]))
            GE(p4_g)
            kv_load.append(sWG.n)

            for j in range(TC):
                kw = (j + 1) * 128

                def p4_te_s(te, h=h, hb=hb, j=j, kw=kw):
                    if j == 0:
                        te.wait_ge(sWG.h, kv_load[h])
                        if h >= 1:
                            te.wait_ge(sSC.h, avev_done[(h - 1, 3)])
                    if (h, j) != (0, 0):
                        pj, ph = (j - 1, h) if j > 0 else (TC - 1, h - 1)
                        te.wait_ge(sSC.h, exp_done[(ph, pj)])
                    for dh in range(2):
                        lhs = qTh[:, hb * 2048 + dh * 1024 + j * 128:
                                  hb * 2048 + dh * 1024 + (j + 1) * 128]
                        for r in range(2):
                            base = hb * 4096 + dh * 2048 + r * 1024
                            off = 0
                            while off < kw:
                                w = min(512, kw - off)
                                te.matmul(ps_s[:, r * 1024 + off:r * 1024 + off + w],
                                          lhs, kT[:, base + off:base + off + w],
                                          start=(dh == 0), stop=False,
                                          skip_group_check=True)
                                off += w
                    ins = None
                    for r in range(2):
                        ins = te.matmul(
                            ps_s[:, r * 1024 + j * 128:r * 1024 + (j + 1) * 128],
                            ident[:],
                            maskb[:, j * 256 + r * 128:j * 256 + (r + 1) * 128],
                            start=False, stop=(r == 1), skip_group_check=True)
                    sTE.inc(ins)
                TE(p4_te_s)
                s_done = sTE.n

                def p4_v1(v, kw=kw, s_done=s_done):
                    v.wait_ge(sTE.h, s_done)
                    v.reduce_max(st[:, 8:9], ps_s[:, 0:kw], axis=AX.X)
                    k1 = sV.inc(v.reduce_max(st[:, 9:10], ps_s[:, 1024:1024 + kw], axis=AX.X))
                    v.wait_ge(sV.h, k1)
                    k2 = sV.inc(v.tensor_max(st[:, 10:11], st[:, 8:9], st[:, 9:10]))
                    v.wait_ge(sV.h, k2)
                    sV.inc(v.tensor_scalar_mul(st[:, 11:12], st[:, 10:11], -0.0625))
                VE(p4_v1)
                mneg_ready = sV.n

                def p4_sc1(sc, kw=kw, mneg_ready=mneg_ready):
                    sc.wait_ge(sV.h, mneg_ready)
                    sc.activation(Psb[:, 0:kw], ps_s[:, 0:kw], AF.Exp,
                                  bias=st[:, 11:12], scale=0.0625,
                                  accum_out=st[:, 12:13])
                    sSC.inc(sc.activation(Psb[:, 1024:1024 + kw],
                                          ps_s[:, 1024:1024 + kw], AF.Exp,
                                          bias=st[:, 11:12], scale=0.0625,
                                          accum_out=st[:, 13:14]))
                SC(p4_sc1)
                exp_done[(h, j)] = sSC.n

                def p4_v2(v, h=h, j=j):
                    v.wait_ge(sSC.h, exp_done[(h, j)])
                    k1 = sV.inc(v.tensor_add(st[:, 14:15], st[:, 12:13], st[:, 13:14]))
                    v.wait_ge(sV.h, k1)
                    k2 = sV.inc(v.reciprocal(st[:, 15:16], st[:, 14:15]))
                    v.wait_ge(sV.h, k2)
                    sV.inc(v.tensor_scalar_mul(diag[:], ident[:], st[:, 15:16]))
                VE(p4_v2)
                diag_ready = sV.n

                first_grp = True
                for r in range(2):
                    u = 0
                    while u <= j:
                        nt = min(4, j + 1 - u)
                        bank_i = ptg % 2

                        def p4_te_pt(te, j=j, r=r, u=u, nt=nt, bank_i=bank_i,
                                     first_grp=first_grp, diag_ready=diag_ready):
                            if first_grp:
                                te.wait_ge(sV.h, diag_ready)
                            if len(ptev_list) >= 2:
                                te.wait_ge(sV.h, ptev_list[-2])
                            bank = ps_pt[bank_i]
                            ins = None
                            for q in range(nt):
                                ins = te.matmul(
                                    bank[:, q * 128:(q + 1) * 128],
                                    Psb[:, r * 1024 + (u + q) * 128:
                                        r * 1024 + (u + q + 1) * 128],
                                    diag[:], start=True, stop=True)
                            sTE.inc(ins)
                        TE(p4_te_pt)
                        ptk = sTE.n

                        def p4_v3(v, j=j, r=r, u=u, nt=nt, bank_i=bank_i, ptk=ptk):
                            v.wait_ge(sTE.h, ptk)
                            bank = ps_pt[bank_i]
                            dst = PT[:, :].rearrange("p (t q) -> p t q", t=16)[
                                :, r * 8 + u:r * 8 + u + nt, j * 128:(j + 1) * 128]
                            src = bank[:, 0:nt * 128].rearrange("p (a b) -> p a b", a=nt)
                            sV.inc(v.tensor_copy(dst, src))
                        VE(p4_v3)
                        ptev_list.append(sV.n)
                        ptg += 1
                        first_grp = False
                        u += nt

            for gq in range(2):
                for dv in range(2):
                    gd = 2 * gq + dv

                    def p4_te_av(te, h=h, hb=hb, gq=gq, dv=dv, gd=gd):
                        if gd == 0:
                            te.wait_ge(sV.h, ptev_list[-1])
                            if h >= 1:
                                te.wait_ge(sSC.h, avev_done[(h - 1, 2)])
                                te.wait_ge(sSC.h, avev_done[(h - 1, 3)])
                        if gd >= 2:
                            te.wait_ge(sSC.h, avev_done[(h, gd - 2)])
                        bank = ps_av[gd % 2]
                        first = True
                        ins = None
                        for r in range(2):
                            for u in range(4 * gq + 4):
                                q0 = max(u, 4 * gq)
                                lhs = vhb[:, hb * 4096 + r * 2048 + u * 256 + dv * 128:
                                          hb * 4096 + r * 2048 + u * 256 + (dv + 1) * 128]
                                rhs = PT[:, (r * 8 + u) * 1024 + q0 * 128:
                                         (r * 8 + u) * 1024 + (4 * gq + 4) * 128]
                                ins = te.matmul(bank[:, (q0 - 4 * gq) * 128:512],
                                                lhs, rhs, start=first, stop=False,
                                                skip_group_check=True)
                                first = False
                        sTE.inc(ins)
                    TE(p4_te_av)
                    av_mm = sTE.n

                    def p4_sc_av(sc, h=h, gd=gd, av_mm=av_mm):
                        sc.wait_ge(sTE.h, av_mm)
                        key = (h, gd - 2) if gd >= 2 else ((h - 1, gd + 2) if h >= 1 else None)
                        if key is not None:
                            sc.wait_ge(sSTG.h, atw_done[key])
                        sSC.inc(sc.copy(
                            atb[:, (gd % 2) * 512:(gd % 2 + 1) * 512],
                            ps_av[gd % 2][:, :]))
                    SC(p4_sc_av)
                    avev_done[(h, gd)] = sSC.n

                    def p4_sy_av(sync, h=h, gq=gq, dv=dv, gd=gd):
                        sync.wait_ge(sSC.h, avev_done[(h, gd)])
                        sSTG.dinc(sync.dma_start(
                            out=attnT_d[h * 256 + dv * 128:h * 256 + (dv + 1) * 128,
                                        gq * 512:(gq + 1) * 512],
                            in_=atb[:, (gd % 2) * 512:(gd % 2 + 1) * 512]))
                    SY(p4_sy_av)
                    atw_done[(h, gd)] = sSTG.n

            head_te_end.append(sTE.n)

        ATW_END = sSTG.n
        P4_SC_END = sSC.n
        P4_V_END = sV.n
        P4_TE_END = sTE.n

        for g in reversed(ps_av_g):
            g.__exit__(None, None, None)
        for g in reversed(ps_pt_g):
            g.__exit__(None, None, None)
        ps_s_g.__exit__(None, None, None)
        al.free("atb")
        al.free("diag")
        al.free("PT")
        al.free("Psb")
        al.free("vhb")
        al.free("qTh")
        al.free("kT")

        blk2.__exit__(None, None, None)

        blk3 = nc.Block("b3")
        blk3.__enter__()
        TE, VE, SC, GE, SY = blk3.tensor, blk3.vector, blk3.scalar, blk3.gpsimd, blk3.sync

        a2tb = al.get("a2tb", [128, DC * 128], BF16)
        b2b = al.get("b2b", [128, D], BF16)
        wr8b = al.get("wr8b", [8, D], BF16)
        atld = al.get("atld", [128, 2 * D], BF16)
        woblk = al.get("woblk", [128, 2 * DC * 512], BF16)
        ysb = al.get("ysb", [128, 4 * 512], F32)

        def t0_g(g):
            g.wait_ge(sTE.h, P4_TE_END)
            g.wait_ge(sV.h, P4_V_END)
            g.wait_ge(sSC.h, P4_SC_END)
            g.wait_ge(sSTG.h, ATW_END)
            sWG.dinc(g.dma_start(out=a2tb[:, :].rearrange("p (c o) -> p c o", c=DC),
                                 in_=a2t_in.rearrange("(c p) o -> p c o", p=128)[:, :, :]))
            sWG.dinc(g.dma_start(out=b2b[:], in_=b2_in[:, :]))
            sWG.dinc(g.dma_start(out=wr8b[:], in_=wr_in[:, :]))
        GE(t0_g)
        FFN_G = sWG.n
        ps5_g = [nc.psum_tensor(f"ps5_{i}", [128, 512], F32) for i in range(8)]
        ps5 = [g.__enter__() for g in ps5_g]

        wo_load, at_load = [], {}
        wo_mm, wo_ev, y_w = {}, {}, {}

        for ob in range(8):
            def p5_g(g, ob=ob):
                if ob == 0:
                    g.wait_ge(sTE.h, P4_TE_END)
                    g.wait_ge(sV.h, P4_V_END)
                    g.wait_ge(sSC.h, P4_SC_END)
                if ob >= 2:
                    g.wait_ge(sTE.h, wo_mm[(ob - 2, TC - 1)])
                src = w_in["wo"][:, ob * 512:(ob + 1) * 512].rearrange(
                    "(c p) o -> p c o", p=128)
                dst = woblk[:, (ob % 2) * DC * 512:(ob % 2 + 1) * DC * 512].rearrange(
                    "p (c o) -> p c o", c=DC)
                sWG.dinc(g.dma_start(out=dst, in_=src[:, :, :]))
            GE(p5_g)
            wo_load.append(sWG.n)

            for tk in range(TC):
                def p5_at(sync, ob=ob, tk=tk):
                    if ob == 0 and tk == 0:
                        sync.wait_ge(sSTG.h, ATW_END)
                        sync.wait_ge(sTE.h, P4_TE_END)
                    prev = (ob, tk - 2) if tk >= 2 else ((ob - 1, tk + 6) if ob >= 1 else None)
                    if prev is not None:
                        sync.wait_ge(sTE.h, wo_mm[prev])
                    src = attnT_d[:, tk * 128:(tk + 1) * 128].rearrange(
                        "(c p) q -> p c q", p=128)
                    dst = atld[:, (tk % 2) * D:(tk % 2 + 1) * D].rearrange(
                        "p (c q) -> p c q", c=DC)
                    sXQ.dinc(sync.dma_start(out=dst, in_=src[:, :, :]))
                SY(p5_at)
                at_load[(ob, tk)] = sXQ.n

                def p5_te(te, ob=ob, tk=tk):
                    te.wait_ge(sWG.h, wo_load[ob])
                    te.wait_ge(sXQ.h, at_load[(ob, tk)])
                    if ob >= 1:
                        te.wait_ge(sEVV.h if tk % 2 == 0 else sEVS.h,
                                   wo_ev[(ob - 1, tk)])
                    bank = ps5[tk]
                    ins = None
                    for c in range(DC):
                        lhs = atld[:, (tk % 2) * D + c * 128:(tk % 2) * D + (c + 1) * 128]
                        rhs = woblk[:, (ob % 2) * DC * 512 + c * 512:
                                    (ob % 2) * DC * 512 + (c + 1) * 512]
                        ins = te.matmul(bank[:, :], lhs, rhs,
                                        start=(c == 0), stop=(c == DC - 1))
                    sTE.inc(ins)
                TE(p5_te)
                wo_mm[(ob, tk)] = sTE.n

                nsl = (ob * TC + tk) % 4

                def p5_ev(eng, ob=ob, tk=tk, nsl=nsl):
                    eng.wait_ge(sTE.h, wo_mm[(ob, tk)])
                    if ob * TC + tk >= 4:
                        eng.wait_ge(sSTG.h, y_w[ob * TC + tk - 4])
                    sem = sEVV if tk % 2 == 0 else sEVS
                    cp = eng.tensor_copy if tk % 2 == 0 else eng.copy
                    sem.inc(cp(ysb[:, nsl * 512:(nsl + 1) * 512], ps5[tk][:, :]))
                (VE if tk % 2 == 0 else SC)(p5_ev)
                wo_ev[(ob, tk)] = (sEVV if tk % 2 == 0 else sEVS).n

                def p5_yw(sync, ob=ob, tk=tk, nsl=nsl):
                    sync.wait_ge(sEVV.h if tk % 2 == 0 else sEVS.h, wo_ev[(ob, tk)])
                    sSTG.dinc(sync.dma_start(
                        out=y_d[tk * 128:(tk + 1) * 128, ob * 512:(ob + 1) * 512],
                        in_=ysb[:, nsl * 512:(nsl + 1) * 512]))
                SY(p5_yw)
                y_w[ob * TC + tk] = sSTG.n

        P5_TE_END = sTE.n

        for g in reversed(ps5_g):
            g.__exit__(None, None, None)
        al.free("ysb")
        al.free("woblk")
        al.free("atld")

        ybuf = al.get("ybuf", [128, D], F32)
        xb2 = al.get("xb2", [128, D], F32)
        x1sb = al.get("x1sb", [128, D], F32)
        ln2c = al.get("ln2c", [128, D], BF16)
        l2t = al.get("l2t", [128, D], BF16)
        fsb = al.get("fsb", [128, D], F32)
        outb = al.get("outb", [128, D], F32)
        lowt = al.get("lowt", [128, 128], BF16)
        rrb = al.get("rrb", [8, 128], BF16)
        rtf = al.get("rtf", [128, 8], F32)
        ex8 = al.get("ex8", [128, 8], F32)
        scb = al.get("scb", [128, 8], BF16)
        scTb = al.get("scTb", [8, 128], BF16)
        lowS = al.get("lowS", [128, 128], BF16)
        psT_g = [nc.psum_tensor(f"psT_{i}", [128, 512], F32) for i in range(8)]
        psT = [g.__enter__() for g in psT_g]

        P5_EVV_END = sEVV.n
        P5_EVS_END = sEVS.n

        wrt_ev2 = []
        for c in range(DC):
            def w_te(te, c=c):
                if c == 0:
                    te.wait_ge(sWG.h, FFN_G)
                    te.wait_ge(sEVV.h, P5_EVV_END)
                    te.wait_ge(sEVS.h, P5_EVS_END)
                else:
                    te.wait_ge(sEVV.h, wrt_ev2[-1])
                sTE.inc(te.matmul(psT[2][:, 0:8], wr8b[0:8, c * 128:(c + 1) * 128],
                                  i8b[:, :], start=True, stop=True))
            TE(w_te)
            wtp = sTE.n

            def w_v(v, c=c, wtp=wtp):
                v.wait_ge(sTE.h, wtp)
                sEVV.inc(v.tensor_copy(wrtb[:, c * 8:(c + 1) * 8], psT[2][:, 0:8]))
            VE(w_v)
            wrt_ev2.append(sEVV.n)
        WRTB_EV_END = sEVV.n

        norm1_hist, x1add_hist, outadd_hist, out_w = [], [], [], []
        prev_l2t_ev = None
        prev_tp_end = None
        b2_free = []
        ffn_ev = {}

        for tk in range(TC):
            def t_sy1(sync, tk=tk):
                if tk == 0:
                    sync.wait_ge(sTE.h, P5_TE_END)
                    sync.wait_ge(sEVV.h, P5_EVV_END)
                    sync.wait_ge(sEVS.h, P5_EVS_END)
                if tk >= 1:
                    sync.wait_ge(sV.h, norm1_hist[tk - 1])
                    sync.wait_ge(sG.h, x1add_hist[tk - 1])
                sync.wait_ge(sSTG.h, y_w[7 * TC + tk])
                sXQ.dinc(sync.dma_start(out=ybuf[:, :],
                                        in_=y_d[tk * 128:(tk + 1) * 128, :]))
                sXQ.dinc(sync.dma_start(out=xb2[:, :],
                                        in_=x_in[tk * 128:(tk + 1) * 128, :]))
            SY(t_sy1)
            yx_load = sXQ.n

            ys = ybuf[:, :]
            xs = xb2[:, :]
            x1s = x1sb[:, :]
            l2s = ln2c[:, :]
            os_ = outb[:, :]

            def t_v1(v, tk=tk, ys=ys, x1s=x1s, yx_load=yx_load):
                v.wait_ge(sXQ.h, yx_load)
                v.tensor_scalar(fsb[:], ys, 1.0, 0.0, ALU.mult, ALU.add, accum_out=st[:, 0:1])
                kacc = sV.inc(v.scalar_tensor_tensor(fsb[:], ys, 1.0, ys, ALU.mult,
                                                     ALU.mult, accum_out=st[:, 1:2]))
                v.wait_ge(sV.h, kacc)
                v.tensor_scalar_mul(st[:, 2:3], st[:, 0:1], 1.0 / D)
                v.tensor_scalar_mul(st[:, 5:6], st[:, 1:2], 1.0 / D)
                kmu = sV.inc(v.tensor_mul(st[:, 6:7], st[:, 2:3], st[:, 2:3]))
                v.wait_ge(sV.h, kmu)
                sV.inc(v.tensor_sub(st[:, 3:4], st[:, 5:6], st[:, 6:7]))
            VE(t_v1)
            var1 = sV.n

            def t_sc1(sc, var1=var1):
                sc.wait_ge(sV.h, var1)
                k = sSC.inc(sc.activation(st[:, 7:8], st[:, 3:4], AF.Ln, bias=st[:, 30:31]))
                sc.wait_ge(sSC.h, k)
                sSC.inc(sc.activation(st[:, 4:5], st[:, 7:8], AF.Exp, bias=st[:, 31:32], scale=-0.5))
            SC(t_sc1)
            rstd1 = sSC.n

            def t_v2(v, tk=tk, ys=ys, x1s=x1s, rstd1=rstd1):
                v.wait_ge(sSC.h, rstd1)
                if tk >= 1:
                    v.wait_ge(sG.h, outadd_hist[tk - 1])
                sV.inc(v.tensor_scalar(x1s, ys, st[:, 2:3], st[:, 4:5],
                                       ALU.subtract, ALU.mult))
            VE(t_v2)
            norm1_hist.append(sV.n)

            def t_g1(g, xs=xs, x1s=x1s, n1=norm1_hist[-1]):
                g.wait_ge(sV.h, n1)
                sG.inc(g.tensor_add(x1s, x1s, xs))
            GE(t_g1)
            x1add_hist.append(sG.n)
            x1_done = sG.n

            def t_v3(v, x1s=x1s, x1_done=x1_done):
                v.wait_ge(sG.h, x1_done)
                v.tensor_scalar(fsb[:], x1s, 1.0, 0.0, ALU.mult, ALU.add, accum_out=st[:, 0:1])
                kacc = sV.inc(v.scalar_tensor_tensor(fsb[:], x1s, 1.0, x1s, ALU.mult,
                                                     ALU.mult, accum_out=st[:, 1:2]))
                v.wait_ge(sV.h, kacc)
                v.tensor_scalar_mul(st[:, 2:3], st[:, 0:1], 1.0 / D)
                v.tensor_scalar_mul(st[:, 5:6], st[:, 1:2], 1.0 / D)
                kmu = sV.inc(v.tensor_mul(st[:, 6:7], st[:, 2:3], st[:, 2:3]))
                v.wait_ge(sV.h, kmu)
                sV.inc(v.tensor_sub(st[:, 3:4], st[:, 5:6], st[:, 6:7]))
            VE(t_v3)
            var2 = sV.n

            def t_sc2(sc, var2=var2):
                sc.wait_ge(sV.h, var2)
                k = sSC.inc(sc.activation(st[:, 7:8], st[:, 3:4], AF.Ln, bias=st[:, 30:31]))
                sc.wait_ge(sSC.h, k)
                sSC.inc(sc.activation(st[:, 4:5], st[:, 7:8], AF.Exp, bias=st[:, 31:32], scale=-0.5))
            SC(t_sc2)
            rstd2 = sSC.n

            def t_v4(v, tk=tk, x1s=x1s, l2s=l2s, rstd2=rstd2):
                v.wait_ge(sSC.h, rstd2)
                if prev_tp_end is not None:
                    v.wait_ge(sTE.h, prev_tp_end)
                sV.inc(v.tensor_scalar(l2s, x1s, st[:, 2:3], st[:, 4:5],
                                       ALU.subtract, ALU.mult))
            VE(t_v4)
            ln2_done = sV.n

            l2t_ev = []
            for gi in range(8):
                def t_te1(te, tk=tk, gi=gi, l2s=l2s, ln2_done=ln2_done,
                          prev_ev=prev_l2t_ev):
                    if tk == 0 and gi == 0:
                        te.wait_ge(sEVV.h, P5_EVV_END)
                        te.wait_ge(sEVS.h, P5_EVS_END)
                    if gi == 0:
                        te.wait_ge(sV.h, ln2_done)
                    if len(l2t_ev) >= 2:
                        te.wait_ge(sEVS.h, l2t_ev[-2])
                    elif prev_ev is not None:
                        te.wait_ge(sEVS.h, prev_ev[gi - 2])
                    bank = psT[gi % 2]
                    ins = None
                    for u in range(4):
                        c = 4 * gi + u
                        ins = te.matmul(bank[:, u * 128:(u + 1) * 128],
                                        l2s[:, c * 128:(c + 1) * 128], ident[:],
                                        start=True, stop=True)
                    sTE.inc(ins)
                TE(t_te1)
                tpd = sTE.n

                def t_sc3(sc, gi=gi, tpd=tpd):
                    sc.wait_ge(sTE.h, tpd)
                    bank = psT[gi % 2]
                    dst = l2t[:, :].rearrange("p (c t) -> p c t", c=DC)[
                        :, 4 * gi:4 * gi + 4, 0:128]
                    src = bank[:, :].rearrange("p (a b) -> p a b", a=4)
                    sEVS.inc(sc.copy(dst, src))
                SC(t_sc3)
                l2t_ev.append(sEVS.n)
            prev_l2t_ev = l2t_ev
            prev_tp_end = sTE.n

            def t_te2(te, tk=tk, lev=l2t_ev[-1]):
                te.wait_ge(sEVS.h, lev)
                if tk == 0:
                    te.wait_ge(sEVV.h, WRTB_EV_END)
                if b2_free:
                    te.wait_ge(sV.h, b2_free[-1])
                ins = None
                for c in range(DC):
                    ins = te.matmul(psT[2][:, 0:128], a2tb[:, c * 128:(c + 1) * 128],
                                    l2t[:, c * 128:(c + 1) * 128],
                                    start=(c == 0), stop=(c == DC - 1),
                                    skip_group_check=True)
                for c in range(DC):
                    ins = te.matmul(psT[2][0:8, 128:256], wrtb[:, c * 8:(c + 1) * 8],
                                    l2t[:, c * 128:(c + 1) * 128],
                                    start=(c == 0), stop=False,
                                    skip_group_check=True)
                ins = te.matmul(psT[2][0:8, 128:256], brb[:, :], ones1[:, :],
                                start=False, stop=True, skip_group_check=True)
                sTE.inc(ins)
            TE(t_te2)
            rl_mm = sTE.n

            def t_v5(v, rl_mm=rl_mm):
                v.wait_ge(sTE.h, rl_mm)
                v.tensor_copy(lowt[:], psT[2][:, 0:128])
                sV.inc(v.tensor_copy(rrb[:], psT[2][0:8, 128:256]))
            VE(t_v5)
            b2_free.append(sV.n)

            def t_te3(te, b2f=b2_free[-1]):
                te.wait_ge(sV.h, b2f)
                sTE.inc(te.matmul(psT[3][:, 0:8], rrb[:], i8b[:, :],
                                  start=True, stop=True))
            TE(t_te3)
            rt_mm = sTE.n

            def t_v6(v, rt_mm=rt_mm):
                v.wait_ge(sTE.h, rt_mm)
                k1 = sV.inc(v.tensor_copy(rtf[:], psT[3][:, 0:8]))
                v.wait_ge(sV.h, k1)
                k2 = sV.inc(v.reduce_max(st[:, 16:17], rtf[:], axis=AX.X))
                v.wait_ge(sV.h, k2)
                sV.inc(v.tensor_scalar_mul(st[:, 17:18], st[:, 16:17], -1.0))
            VE(t_v6)
            rsm = sV.n

            def t_sc4(sc, rsm=rsm):
                sc.wait_ge(sV.h, rsm)
                sSC.inc(sc.activation(ex8[:], rtf[:], AF.Exp, bias=st[:, 17:18],
                                      accum_out=st[:, 18:19]))
            SC(t_sc4)
            exp8_done = sSC.n

            def t_v7(v, exp8_done=exp8_done):
                v.wait_ge(sSC.h, exp8_done)
                k1 = sV.inc(v.reciprocal(st[:, 19:20], st[:, 18:19]))
                v.wait_ge(sV.h, k1)
                sV.inc(v.tensor_scalar(scb[:], ex8[:], st[:, 19:20], 16.0,
                                       ALU.mult, ALU.mult))
            VE(t_v7)
            scb_done = sV.n

            def t_te4(te, scb_done=scb_done):
                te.wait_ge(sV.h, scb_done)
                sTE.inc(te.matmul(psT[3][0:8, 128:256], scb[:], ident[:],
                                  start=True, stop=True))
            TE(t_te4)
            sct_mm = sTE.n

            def t_v8(v, sct_mm=sct_mm):
                v.wait_ge(sTE.h, sct_mm)
                sV.inc(v.tensor_copy(scTb[:], psT[3][0:8, 128:256]))
            VE(t_v8)
            sct_done = sV.n

            def t_te5(te, sct_done=sct_done):
                te.wait_ge(sV.h, sct_done)
                sTE.inc(te.matmul(psT[3][:, 256:384], e8b[:], scTb[:],
                                  start=True, stop=True))
            TE(t_te5)
            ex_mm = sTE.n

            def t_v9(v, ex_mm=ex_mm):
                v.wait_ge(sTE.h, ex_mm)
                sV.inc(v.tensor_tensor(lowS[:], lowt[:], psT[3][:, 256:384], ALU.mult))
            VE(t_v9)
            lowS_done = sV.n

            for half in range(2):
                def t_te6(te, tk=tk, half=half, lowS_done=lowS_done):
                    if half == 0:
                        te.wait_ge(sV.h, lowS_done)
                    if (tk, half) != (0, 0):
                        pk = (tk, half - 1) if half == 1 else (tk - 1, 1)
                        for eng_sem, val in ffn_ev[pk]:
                            te.wait_ge(eng_sem.h, val)
                    ins = None
                    for og4 in range(4):
                        og = half * 4 + og4
                        ins = te.matmul(psT[4 + og4][:, :], lowS[:],
                                        b2b[:, og * 512:(og + 1) * 512],
                                        start=True, stop=True)
                    sTE.inc(ins)
                TE(t_te6)
                f_mm = sTE.n

                evs = []

                def t_vf(v, half=half, f_mm=f_mm):
                    v.wait_ge(sTE.h, f_mm)
                    v.tensor_copy(fsb[:, (half * 4) * 512:(half * 4 + 1) * 512],
                                  psT[4][:, :])
                    sEVV.inc(v.tensor_copy(
                        fsb[:, (half * 4 + 1) * 512:(half * 4 + 2) * 512],
                        psT[5][:, :]))
                VE(t_vf)
                evs.append((sEVV, sEVV.n))

                def t_sf(sc, half=half, f_mm=f_mm):
                    sc.wait_ge(sTE.h, f_mm)
                    sc.copy(fsb[:, (half * 4 + 2) * 512:(half * 4 + 3) * 512],
                            psT[6][:, :])
                    sEVS.inc(sc.copy(
                        fsb[:, (half * 4 + 3) * 512:(half * 4 + 4) * 512],
                        psT[7][:, :]))
                SC(t_sf)
                evs.append((sEVS, sEVS.n))
                ffn_ev[(tk, half)] = evs

            def t_v10(v, tk=tk, os_=os_):
                for eng_sem, val in ffn_ev[(tk, 0)] + ffn_ev[(tk, 1)]:
                    v.wait_ge(eng_sem.h, val)
                if tk >= 1:
                    v.wait_ge(sOUT.h, out_w[tk - 1])
                v.tensor_scalar(os_, fsb[:], 1.0, 0.0, ALU.mult, ALU.add, accum_out=st[:, 0:1])
                kacc = sV.inc(v.scalar_tensor_tensor(os_, fsb[:], 1.0, fsb[:], ALU.mult,
                                                     ALU.mult, accum_out=st[:, 1:2]))
                v.wait_ge(sV.h, kacc)
                v.tensor_scalar_mul(st[:, 2:3], st[:, 0:1], 1.0 / D)
                v.tensor_scalar_mul(st[:, 5:6], st[:, 1:2], 1.0 / D)
                kmu = sV.inc(v.tensor_mul(st[:, 6:7], st[:, 2:3], st[:, 2:3]))
                v.wait_ge(sV.h, kmu)
                sV.inc(v.tensor_sub(st[:, 3:4], st[:, 5:6], st[:, 6:7]))
            VE(t_v10)
            var3 = sV.n

            def t_sc5(sc, var3=var3):
                sc.wait_ge(sV.h, var3)
                k = sSC.inc(sc.activation(st[:, 7:8], st[:, 3:4], AF.Ln, bias=st[:, 30:31]))
                sc.wait_ge(sSC.h, k)
                sSC.inc(sc.activation(st[:, 4:5], st[:, 7:8], AF.Exp, bias=st[:, 31:32], scale=-0.5))
            SC(t_sc5)
            rstd3 = sSC.n

            def t_v11(v, os_=os_, rstd3=rstd3):
                v.wait_ge(sSC.h, rstd3)
                sV.inc(v.tensor_scalar(os_, fsb[:], st[:, 2:3], st[:, 4:5],
                                       ALU.subtract, ALU.mult))
            VE(t_v11)
            norm3 = sV.n

            def t_g2(g, os_=os_, x1s=x1s, norm3=norm3):
                g.wait_ge(sV.h, norm3)
                sG.inc(g.tensor_add(os_, os_, x1s))
            GE(t_g2)
            outadd_hist.append(sG.n)
            oadd = sG.n

            def t_syo(sync, tk=tk, os_=os_, oadd=oadd):
                sync.wait_ge(sG.h, oadd)
                sOUT.dinc(sync.dma_start(out=out_hbm[tk * 128:(tk + 1) * 128, :],
                                         in_=os_))
            SY(t_syo)
            out_w.append(sOUT.n)

        OUT_END = sOUT.n

        for g in reversed(psT_g):
            g.__exit__(None, None, None)

        def fin_wait(sync):
            sync.wait_ge(sOUT.h, OUT_END)
        SY(fin_wait)

        blk3.__exit__(None, None, None)

        FIN_PRE = sSTG.n
        for arm in nc.switch_core_id([nc.sync], n=128):
            slot = int(arm) % 2
            nc.sync.dma_start(out=SHF[0:1, 2 + slot:3 + slot],
                              in_=mg[0:1, 1:2]).then_inc(sSTG.h, 16)
            nc.sync.wait_ge(sSTG.h, FIN_PRE + 16)
        sSTG.n = FIN_PRE + 16

        blk4 = nc.Block("b4")
        blk4.__enter__()

        def p_final(sync):
            emit_poll(sync, 2, 1, sPOLL2)
            if DEBUG:
                sOUT.dinc(sync.dma_start(out=dbg["k"][:, :], in_=kloc[:, :]))
                sOUT.dinc(sync.dma_start(out=dbg["v"][:, :], in_=vloc[:, :]))
                sOUT.dinc(sync.dma_start(out=dbg["q"][:, :], in_=qloc[:, :]))
                sOUT.dinc(sync.dma_start(out=dbg["at"][:, :], in_=attnT_d[:, :]))
                sOUT.dinc(sync.dma_start(out=dbg["y"][:, :], in_=y_d[:, :]))
                sync.wait_ge(sOUT.h, sOUT.n)
            sync.wait_ge(sOUT.h, OUT_END)
        blk4.sync(p_final)
        blk4.__exit__(None, None, None)

        for side in ("left", "right"):
            while al.stacks[side]:
                _, g = al.stacks[side].pop()
                g.__exit__(None, None, None)

    return nc


_CACHE = {}


def kernel(**inputs):
    import jax
    epoch = next(_EPOCH)
    ready_magic = np.int32(0x5AFE0000 + 2 * epoch)
    fin_magic = np.int32(0x5AFE0000 + 2 * epoch + 1)

    devs = jax.devices()[:8]
    hw = [int(getattr(d, "local_hardware_id", i)) for i, d in enumerate(devs)]

    X = np.asarray(inputs["L0_attn_hidden_states"], dtype=np.float32)
    wq = np.asarray(inputs["L0_attn_Wq"], dtype=np.float32)
    wk = np.asarray(inputs["L0_attn_Wk"], dtype=np.float32)
    wv = np.asarray(inputs["L0_attn_Wv"], dtype=np.float32)
    wo = np.asarray(inputs["L0_attn_Wo"], dtype=np.float32)
    A = np.asarray(inputs["L0_ffn_A"], dtype=np.float32)
    Bm = np.asarray(inputs["L0_ffn_B"], dtype=np.float32)
    Wr = np.asarray(inputs["L0_ffn_Wr"], dtype=np.float32)
    br = np.asarray(inputs["L0_ffn_br"], dtype=np.float32)

    a2t = np.ascontiguousarray(A.reshape(128, D).T)
    b2 = np.ascontiguousarray(Bm.transpose(0, 2, 1).reshape(128, D))
    wr8 = np.ascontiguousarray(Wr)
    br8 = np.ascontiguousarray(br.reshape(1, 8))
    magic = np.array([[ready_magic, fin_magic, 0, 0]], np.int32)

    in_maps = []
    for i in range(8):
        p, b = hw[i] % 2, hw[i] // 2
        cos_fm, sin_fm = _rope_tables(p)
        in_maps.append({
            "x_own": np.ascontiguousarray(
                X[b].reshape(16, 128, D)[p::2].reshape(TOK, D)),
            "wq": wq, "wk": wk, "wv": wv, "wo": wo,
            "a2t": a2t, "b2": b2, "wr8": wr8, "br8": br8,
            "cosfm_i": cos_fm, "sinfm_i": sin_fm,
            "maskq": _mask_tables(p),
            "magic": magic,
        })

    nc = build_program()
    res = run_bass_kernel_spmd(nc, in_maps, core_ids=list(range(8)))

    OUT = np.empty((4, 2048, D), np.float32)
    for i in range(8):
        p, b = hw[i] % 2, hw[i] // 2
        OUT[b].reshape(16, 128, D)[p::2] = res.results[i]["out"].reshape(TC, 128, D)
    return OUT


# version 09
# speedup vs baseline: 1.0700x; 1.0700x over previous
import itertools
import numpy as np
import ml_dtypes
from contextlib import ExitStack

import concourse.bass as bass
import concourse.mybir as mybir
from concourse.bass_utils import run_bass_kernel_spmd

F32 = mybir.dt.float32
BF16 = mybir.dt.bfloat16
I32 = mybir.dt.int32
AF = mybir.ActivationFunctionType
ALU = mybir.AluOpType
AX = mybir.AxisListType

D = 4096
DC = 32
TOK = 1024
TC = 8
HEADS = 16
ROT = 64
NEG = -1.0e9
EPS = 1e-5
SHUF = [i + 1 - 2 * (i % 2) for i in range(32)]

_EPOCH = itertools.count(1)
DEBUG = False


def _rope_tables(p):
    inv = (1.0 / 10000.0 ** (np.arange(0, ROT, 2, dtype=np.float32) / ROT)).astype(np.float32)
    tg = np.concatenate([np.arange((2 * j + p) * 128, (2 * j + p + 1) * 128) for j in range(TC)])
    ang = tg[:, None].astype(np.float32) * inv[None, :]
    cos_fm = np.ascontiguousarray(np.cos(ang).T[np.repeat(np.arange(32), 2)]).astype(np.float32)
    sin_fm = np.ascontiguousarray(np.sin(ang).T[np.repeat(np.arange(32), 2)]).astype(np.float32)
    return cos_fm, sin_fm


def _mask_tables(p):
    tri = np.where(np.arange(128)[:, None] >= np.arange(128)[None, :], 0.0, NEG).astype(np.float32)
    m = np.zeros((TC, 128, 256), np.float32)
    for j in range(TC):
        if p == 0:
            m[j, :, 0:128] = tri
            m[j, :, 128:256] = NEG
        else:
            m[j, :, 128:256] = tri
    return m


class Sem:
    def __init__(self, nc, ctx, name):
        self.h = ctx.enter_context(nc.semaphore(name))
        self.n = 0

    def inc(self, instr, k=1):
        instr.then_inc(self.h, k)
        self.n += k
        return self.n

    def dinc(self, instr):
        return self.inc(instr, 16)


class Alloc:

    def __init__(self, nc):
        self.nc = nc
        self.stacks = {"left": [], "right": []}

    def get(self, name, shape, dtype, side="left"):
        g = self.nc.sbuf_tensor(name, shape, dtype, side=side)
        h = g.__enter__()
        self.stacks[side].append((name, g))
        return h

    def free(self, name, side="left"):
        nm, g = self.stacks[side].pop()
        assert nm == name, (nm, name)
        g.__exit__(None, None, None)


def build_program():
    nc = bass.Bass("TRN2", target_bir_lowering=False)

    x_in = nc.dram_tensor("x_own", [TOK, D], F32, kind="ExternalInput").ap()
    w_in = {m: nc.dram_tensor(m, [D, D], F32, kind="ExternalInput").ap()
            for m in ("wq", "wk", "wv", "wo")}
    a2t_in = nc.dram_tensor("a2t", [D, 128], F32, kind="ExternalInput").ap()
    b2_in = nc.dram_tensor("b2", [128, D], F32, kind="ExternalInput").ap()
    wr_in = nc.dram_tensor("wr8", [8, D], F32, kind="ExternalInput").ap()
    br_in = nc.dram_tensor("br8", [1, 8], F32, kind="ExternalInput").ap()
    cosfm_in = nc.dram_tensor("cosfm_i", [ROT, TOK], F32, kind="ExternalInput").ap()
    sinfm_in = nc.dram_tensor("sinfm_i", [ROT, TOK], F32, kind="ExternalInput").ap()
    maskq_in = nc.dram_tensor("maskq", [TC, 128, 256], F32, kind="ExternalInput").ap()
    magic_in = nc.dram_tensor("magic", [1, 4], I32, kind="ExternalInput").ap()

    out_hbm = nc.dram_tensor("out", [TOK, D], F32, kind="ExternalOutput").ap()
    dbg = {}
    if DEBUG:
        dbg["k"] = nc.dram_tensor("dbg_k", [D, TOK], BF16, kind="ExternalOutput").ap()
        dbg["v"] = nc.dram_tensor("dbg_v", [TOK, D], BF16, kind="ExternalOutput").ap()
        dbg["q"] = nc.dram_tensor("dbg_q", [D, TOK], BF16, kind="ExternalOutput").ap()
        dbg["at"] = nc.dram_tensor("dbg_at", [TOK, D], BF16, kind="ExternalOutput").ap()
        dbg["y"] = nc.dram_tensor("dbg_y", [TOK, D], F32, kind="ExternalOutput").ap()

    KSH = nc.dram_tensor("KSH", [2 * D, TOK], BF16, kind="Internal", addr_space="Shared").ap()
    VSH = nc.dram_tensor("VSH", [2 * TOK, D], BF16, kind="Internal", addr_space="Shared").ap()
    SHF = nc.dram_tensor("SHF", [1, 8], I32, kind="Internal", addr_space="Shared").ap()
    kloc = nc.dram_tensor("kloc", [D, TOK], BF16, kind="Internal").ap()
    qloc = nc.dram_tensor("qloc", [D, TOK], BF16, kind="Internal").ap()
    vloc = nc.dram_tensor("vloc", [TOK, D], BF16, kind="Internal").ap()
    attnT_d = nc.dram_tensor("attnT_d", [TOK, D], BF16, kind="Internal").ap()
    y_d = nc.dram_tensor("y_d", [TOK, D], F32, kind="Internal").ap()

    i128c = nc.inline_tensor(np.eye(128, dtype=ml_dtypes.bfloat16), "i128c").ap()
    i8cc = nc.inline_tensor(np.eye(8, dtype=ml_dtypes.bfloat16), "i8cc").ap()
    ones_c = nc.inline_tensor(np.ones((1, 128), dtype=ml_dtypes.bfloat16), "ones_cc").ap()
    e8_np = np.zeros((8, 128), dtype=ml_dtypes.bfloat16)
    for h in range(8):
        e8_np[h, h * 16:(h + 1) * 16] = 1.0
    e8c = nc.inline_tensor(e8_np, "e8c").ap()

    ctx = ExitStack()
    with ctx:
        en = ctx.enter_context
        al = Alloc(nc)

        ident = en(nc.sbuf_tensor("ident", [128, 128], BF16))
        i8b = en(nc.sbuf_tensor("i8b", [8, 8], BF16))
        e8b = en(nc.sbuf_tensor("e8b", [8, 128], BF16))
        ones1 = en(nc.sbuf_tensor("ones1", [1, 128], BF16))
        maskb = en(nc.sbuf_tensor("maskb", [128, TC * 256], BF16))
        cosfm = en(nc.sbuf_tensor("cosfm", [ROT, TOK], F32))
        sinfm = en(nc.sbuf_tensor("sinfm", [ROT, TOK], F32))
        wrtb = en(nc.sbuf_tensor("wrtb", [128, DC * 8], BF16))
        brb = en(nc.sbuf_tensor("brb", [1, 8], BF16))
        mg = en(nc.sbuf_tensor("mg", [1, 4], I32))
        fl = en(nc.sbuf_tensor("fl", [1, 4], I32))
        st = en(nc.sbuf_tensor("st", [128, 32], F32))

        sXQ = Sem(nc, ctx, "sXQ")
        sWG = Sem(nc, ctx, "sWG")
        sSTG = Sem(nc, ctx, "sSTG")
        sOUT = Sem(nc, ctx, "sOUT")
        sTE = Sem(nc, ctx, "sTE")
        sV = Sem(nc, ctx, "sV")
        sSC = Sem(nc, ctx, "sSC")
        sEVV = Sem(nc, ctx, "sEVV")
        sEVS = Sem(nc, ctx, "sEVS")
        sG = Sem(nc, ctx, "sG")
        sPOLL = Sem(nc, ctx, "sPOLL")
        sPOLL2 = Sem(nc, ctx, "sPOLL2")

        blk1 = nc.Block("b1")
        blk1.__enter__()
        TE, VE, SC, GE, SY = blk1.tensor, blk1.vector, blk1.scalar, blk1.gpsimd, blk1.sync

        def p0_g(g):
            sWG.dinc(g.dma_start(out=ident[:], in_=i128c[:, :]))
            sWG.dinc(g.dma_start(out=i8b[:], in_=i8cc[:, :]))
            sWG.dinc(g.dma_start(out=e8b[:], in_=e8c[:, :]))
            sWG.dinc(g.dma_start(out=ones1[:], in_=ones_c[:, :]))
            sWG.dinc(g.dma_start(out=maskb[:, :].rearrange("p (j k) -> p j k", j=TC), in_=maskq_in.rearrange("j p k -> p j k")[:, :, :]))
            sWG.dinc(g.dma_start(out=brb[:], in_=br_in[:, :]))
        GE(p0_g)
        CONSTS_G = sWG.n

        def p0_v(v):
            v.memset(st[:, 30:31], EPS)
            v.memset(st[:, 31:32], 0.0)
        VE(p0_v)

        def p0_s(sync):
            sXQ.dinc(sync.dma_start(out=mg[:], in_=magic_in[:, :]))
            sXQ.dinc(sync.dma_start(out=cosfm[:], in_=cosfm_in[:, :]))
            sXQ.dinc(sync.dma_start(out=sinfm[:], in_=sinfm_in[:, :]))
        SY(p0_s)

        WRT_EV_END = sEVV.n

        ln0T = al.get("ln0T", [128, DC * TOK], BF16)

        def ln0T_c(c):
            return ln0T[:, c * TOK:(c + 1) * TOK]

        xbuf = al.get("xbuf", [128, 2 * D], F32)
        ln0c = al.get("ln0c", [128, 2 * D], BF16)
        ps_tp_g = [nc.psum_tensor(f"ps_tp{i}", [128, 512], F32) for i in range(2)]
        ps_tp = [g.__enter__() for g in ps_tp_g]

        xload, ln_done, tp_ev = [], [], []
        for j in range(TC):
            xs = xbuf[:, (j % 2) * D:(j % 2 + 1) * D]
            lns = ln0c[:, (j % 2) * D:(j % 2 + 1) * D]

            def p1_sy(sync, j=j, xs=xs):
                if j >= 2:
                    sync.wait_ge(sV.h, ln_done[j - 2])
                sXQ.dinc(sync.dma_start(out=xs, in_=x_in[j * 128:(j + 1) * 128, :]))
            SY(p1_sy)
            xload.append(sXQ.n)

            def p1_v1(v, j=j, xs=xs, lns=lns):
                v.wait_ge(sXQ.h, xload[j])
                v.tensor_scalar(lns, xs, 1.0, 0.0, ALU.mult, ALU.add, accum_out=st[:, 0:1])
                kacc = sV.inc(v.scalar_tensor_tensor(lns, xs, 1.0, xs, ALU.mult,
                                                     ALU.mult, accum_out=st[:, 1:2]))
                v.wait_ge(sV.h, kacc)
                v.tensor_scalar_mul(st[:, 2:3], st[:, 0:1], 1.0 / D)
                v.tensor_scalar_mul(st[:, 5:6], st[:, 1:2], 1.0 / D)
                kmu = sV.inc(v.tensor_mul(st[:, 6:7], st[:, 2:3], st[:, 2:3]))
                v.wait_ge(sV.h, kmu)
                sV.inc(v.tensor_sub(st[:, 3:4], st[:, 5:6], st[:, 6:7]))
            VE(p1_v1)
            var_ready = sV.n

            def p1_sc1(sc, var_ready=var_ready):
                sc.wait_ge(sV.h, var_ready)
                sSC.inc(sc.activation(st[:, 7:8], st[:, 3:4], AF.Sqrt, bias=st[:, 30:31]))
            SC(p1_sc1)
            rstd_ready = sSC.n

            def p1_v2(v, xs=xs, lns=lns, rstd_ready=rstd_ready):
                v.wait_ge(sSC.h, rstd_ready)
                kr = sV.inc(v.reciprocal(st[:, 4:5], st[:, 7:8]))
                v.wait_ge(sV.h, kr)
                sV.inc(v.tensor_scalar(lns, xs, st[:, 2:3], st[:, 4:5],
                                       ALU.subtract, ALU.mult))
            VE(p1_v2)
            ln_done.append(sV.n)

            for gi in range(8):
                gg = 8 * j + gi

                def p1_te(te, j=j, gi=gi, gg=gg, lns=lns):
                    if gi == 0:
                        te.wait_ge(sV.h, ln_done[j])
                    if gg >= 2:
                        te.wait_ge(sEVS.h, tp_ev[gg - 2])
                    bank = ps_tp[gg % 2]
                    ins = None
                    for u in range(4):
                        c = 4 * gi + u
                        ins = te.matmul(bank[:, u * 128:(u + 1) * 128],
                                        lns[:, c * 128:(c + 1) * 128], ident[:],
                                        start=True, stop=True)
                    sTE.inc(ins)
                TE(p1_te)
                tpd = sTE.n

                def p1_sc2(sc, j=j, gi=gi, gg=gg, tpd=tpd):
                    sc.wait_ge(sTE.h, tpd)
                    bank = ps_tp[gg % 2]
                    dst = ln0T[:, :].rearrange("p (c t) -> p c t", c=DC)[
                        :, 4 * gi:4 * gi + 4, j * 128:(j + 1) * 128]
                    src = bank[:, :].rearrange("p (a b) -> p a b", a=4)
                    sEVS.inc(sc.copy(dst, src))
                SC(p1_sc2)
                tp_ev.append(sEVS.n)

        P1_V_END = sV.n
        P1_EVS_END = sEVS.n
        for g in reversed(ps_tp_g):
            g.__exit__(None, None, None)
        al.free("ln0c")
        al.free("xbuf")

        ps4_g = [nc.psum_tensor(f"ps4_{i}", [128, 512], F32) for i in range(4)]
        ps4 = [g.__enter__() for g in ps4_g]

        def emit_fm(kind, w_dram, dest_dram, gate_evv, gate_v=None, gate_evs=None,
                    gate_te=None, gate_stg=None):
            wcol = al.get("wcol" + kind, [128, 2 * D], BF16)
            kqs = al.get("kqs" + kind, [128, 2 * TOK], BF16)
            rsc = al.get("rsc" + kind, [64, 2 * 512], F32)
            wload, mmdone, evdone, stgdone = [], [], [], []
            for oc in range(DC):
                def fm_g(g, oc=oc):
                    if oc == 0:
                        g.wait_ge(sEVV.h, gate_evv)
                        if gate_v is not None:
                            g.wait_ge(sV.h, gate_v)
                        if gate_evs is not None:
                            g.wait_ge(sEVS.h, gate_evs)
                        if gate_te is not None:
                            g.wait_ge(sTE.h, gate_te)
                        if gate_stg is not None:
                            g.wait_ge(sSTG.h, gate_stg)
                    if oc >= 2:
                        g.wait_ge(sTE.h, mmdone[oc - 2])
                    src = w_dram[:, oc * 128:(oc + 1) * 128].rearrange(
                        "(c p) o -> p c o", p=128)
                    dst = wcol[:, (oc % 2) * D:(oc % 2 + 1) * D].rearrange(
                        "p (c o) -> p c o", c=DC)
                    sWG.dinc(g.dma_start(out=dst, in_=src[:, :, :]))
                GE(fm_g)
                wload.append(sWG.n)

                def fm_te(te, oc=oc):
                    if oc == 0:
                        te.wait_ge(sEVS.h, P1_EVS_END)
                        te.wait_ge(sEVV.h, gate_evv)
                    te.wait_ge(sWG.h, wload[oc])
                    if oc >= 2:
                        te.wait_ge(sEVV.h, evdone[oc - 2])
                    ins = None
                    for tg in range(2):
                        bank = ps4[2 * (oc % 2) + tg]
                        for c in range(DC):
                            lhs = wcol[:, (oc % 2) * D + c * 128:(oc % 2) * D + (c + 1) * 128]
                            ins = te.matmul(bank[:, :], lhs,
                                            ln0T_c(c)[:, tg * 512:(tg + 1) * 512],
                                            start=(c == 0), stop=(c == DC - 1))
                    sTE.inc(ins)
                TE(fm_te)
                mmdone.append(sTE.n)

                def fm_v(v, oc=oc):
                    v.wait_ge(sTE.h, mmdone[oc])
                    if oc >= 2:
                        v.wait_ge(sSTG.h, stgdone[oc - 2])
                    dstc = kqs[:, (oc % 2) * TOK:(oc % 2 + 1) * TOK]
                    ins = None
                    for tg in range(2):
                        bank = ps4[2 * (oc % 2) + tg]
                        dst = dstc[:, tg * 512:(tg + 1) * 512]
                        if oc % 2 == 0:
                            qs = rsc[:, 0:512]
                            t1 = rsc[:, 512:1024]
                            v.stream_shuffle(qs, bank[0:64, :], mask=SHUF)
                            v.tensor_tensor(t1, qs, sinfm[:, tg * 512:(tg + 1) * 512], ALU.mult)
                            v.tensor_tensor(qs, bank[0:64, :], cosfm[:, tg * 512:(tg + 1) * 512], ALU.mult)
                            v.tensor_add(dst[0:64, :], qs, t1)
                            ins = v.tensor_copy(dst[64:128, :], bank[64:128, :])
                        else:
                            ins = v.tensor_copy(dst, bank[:, :])
                    sEVV.inc(ins)
                VE(fm_v)
                evdone.append(sEVV.n)

                def fm_sy(sync, oc=oc):
                    sync.wait_ge(sEVV.h, evdone[oc])
                    sSTG.dinc(sync.dma_start(
                        out=dest_dram[oc * 128:(oc + 1) * 128, :],
                        in_=kqs[:, (oc % 2) * TOK:(oc % 2 + 1) * TOK]))
                SY(fm_sy)
                stgdone.append(sSTG.n)
            al.free("rsc" + kind)
            al.free("kqs" + kind)
            al.free("wcol" + kind)
            return mmdone, evdone, stgdone

        k_mm, k_ev, k_stg = emit_fm('K', w_in["wk"], kloc, WRT_EV_END,
                                    gate_v=P1_V_END, gate_evs=P1_EVS_END)

        wvb = al.get("wvb", [128, 2 * DC * 512], BF16)
        vsb = al.get("vsb", [128, 2 * 512], BF16)
        v_wload, v_mm, v_ev, v_stg = [], {}, {}, {}
        for ob in range(8):
            def v_g(g, ob=ob):
                if ob == 0:
                    g.wait_ge(sEVS.h, P1_EVS_END)
                    g.wait_ge(sTE.h, k_mm[-1])
                    g.wait_ge(sEVV.h, k_ev[-1])
                    g.wait_ge(sSTG.h, k_stg[-1])
                if ob >= 2:
                    g.wait_ge(sTE.h, v_mm[(ob - 2, TC - 1)])
                src = w_in["wv"][:, ob * 512:(ob + 1) * 512].rearrange(
                    "(c p) o -> p c o", p=128)
                dst = wvb[:, (ob % 2) * DC * 512:(ob % 2 + 1) * DC * 512].rearrange(
                    "p (c o) -> p c o", c=DC)
                sWG.dinc(g.dma_start(out=dst, in_=src[:, :, :]))
            GE(v_g)
            v_wload.append(sWG.n)

            for tk in range(TC):
                t4 = tk % 4

                def v_te(te, ob=ob, tk=tk, t4=t4):
                    if ob == 0 and tk == 0:
                        te.wait_ge(sEVV.h, k_ev[-1])
                    te.wait_ge(sWG.h, v_wload[ob])
                    prev = (ob, tk - 4) if tk >= 4 else ((ob - 1, tk + 4) if ob >= 1 else None)
                    if prev is not None:
                        te.wait_ge(sEVV.h, v_ev[prev])
                    bank = ps4[t4]
                    ins = None
                    for c in range(DC):
                        lhs = ln0T_c(c)[:, tk * 128:(tk + 1) * 128]
                        rhs = wvb[:, (ob % 2) * DC * 512 + c * 512:
                                  (ob % 2) * DC * 512 + (c + 1) * 512]
                        ins = te.matmul(bank[:, :], lhs, rhs,
                                        start=(c == 0), stop=(c == DC - 1))
                    sTE.inc(ins)
                TE(v_te)
                v_mm[(ob, tk)] = sTE.n

                def v_v(v, ob=ob, tk=tk, t4=t4):
                    v.wait_ge(sTE.h, v_mm[(ob, tk)])
                    nsl = (ob * TC + tk) % 2
                    if ob * TC + tk >= 2:
                        v.wait_ge(sSTG.h, v_stg[ob * TC + tk - 2])
                    sEVV.inc(v.tensor_copy(vsb[:, nsl * 512:(nsl + 1) * 512],
                                           ps4[t4][:, :]))
                VE(v_v)
                v_ev[(ob, tk)] = sEVV.n

                def v_sy(sync, ob=ob, tk=tk):
                    sync.wait_ge(sEVV.h, v_ev[(ob, tk)])
                    nsl = (ob * TC + tk) % 2
                    sSTG.dinc(sync.dma_start(
                        out=vloc[tk * 128:(tk + 1) * 128, ob * 512:(ob + 1) * 512],
                        in_=vsb[:, nsl * 512:(nsl + 1) * 512]))
                SY(v_sy)
                v_stg[ob * TC + tk] = sSTG.n
        V_EV_END = sEVV.n
        al.free("vsb")
        al.free("wvb")

        V_STG_END2 = sSTG.n

        blk1.__exit__(None, None, None)

        for arm in nc.switch_core_id([nc.sync], n=128):
            slot = int(arm) % 2
            nc.sync.wait_ge(sSTG.h, V_STG_END2)
            nc.sync.wait_ge(sXQ.h, 16)
            nc.sync.dma_start(out=KSH[slot * D:(slot + 1) * D, :],
                              in_=kloc[:, :]).then_inc(sSTG.h, 16)
            nc.sync.dma_start(out=VSH[slot * TOK:(slot + 1) * TOK, :],
                              in_=vloc[:, :]).then_inc(sSTG.h, 16)
            nc.sync.wait_ge(sSTG.h, V_STG_END2 + 32)
            nc.sync.dma_start(out=SHF[0:1, slot:slot + 1],
                              in_=mg[0:1, 0:1]).then_inc(sSTG.h, 16)
            nc.sync.wait_ge(sSTG.h, V_STG_END2 + 48)
        sSTG.n = V_STG_END2 + 48

        blk2 = nc.Block("b2")
        blk2.__enter__()
        TE, VE, SC, GE, SY = blk2.tensor, blk2.vector, blk2.scalar, blk2.gpsimd, blk2.sync

        q_mm, q_ev, q_stg = emit_fm('Q', w_in["wq"], qloc, V_EV_END,
                                    gate_te=v_mm[(7, TC - 1)],
                                    gate_stg=v_stg[8 * TC - 1])
        Q_TE_END = sTE.n
        Q_EV_END = sEVV.n
        STG_PRE_ARMS = sSTG.n

        for g in reversed(ps4_g):
            g.__exit__(None, None, None)
        al.free("ln0T")

        def emit_poll(sync, col0, mgcol, sem):
            with sync.register() as r_exp, sync.register() as r_f0, \
                    sync.register() as r_f1, sync.register() as r_c, \
                    sync.register() as r_m:
                sync.load(r_m, mg[0:1, mgcol:mgcol + 1])
                sync.reg_mov(r_exp, 0)

                def cond():
                    sync.dma_start(out=fl[0:1, 0:2],
                                   in_=SHF[0:1, col0:col0 + 2]).then_inc(sem.h, 16)
                    sync.reg_add(r_exp, r_exp, 16)
                    sync.wait_ge(sem.h, r_exp)
                    sync.load(r_f0, fl[0:1, 0:1])
                    sync.load(r_f1, fl[0:1, 1:2])
                    sync.reg_alu(r_c, r_f0, r_m, ALU.is_equal)
                    sync.reg_alu(r_f1, r_f1, r_m, ALU.is_equal)
                    sync.reg_alu(r_c, r_c, r_f1, ALU.bitwise_and)
                    sync.reg_alu(r_c, r_c, 1, ALU.is_lt)
                    return r_c

                with sync.While(cond):
                    sync.nop()

        def p3_sy(sync):
            emit_poll(sync, 0, 0, sPOLL)
            sOUT.inc(sync.nop())
        SY(p3_sy)
        POLL_DONE = sOUT.n

        kT = al.get("kT", [128, 2 * 4096], BF16)
        qTh = al.get("qTh", [128, 2 * 2048], BF16)
        vhb = al.get("vhb", [128, 2 * 4096], BF16)
        Psb = al.get("Psb", [128, 2048], BF16)
        PT = al.get("PT", [128, 2 * 16 * 1024], BF16)
        diag = al.get("diag", [128, 128], BF16)
        atb = al.get("atb", [128, 2 * 512], BF16)
        ps_s_g = nc.psum_tensor("ps_s", [128, 2048], F32)
        ps_s = ps_s_g.__enter__()
        ps_pt_g = [nc.psum_tensor(f"ps_pt{i}", [128, 512], F32) for i in range(2)]
        ps_pt = [g.__enter__() for g in ps_pt_g]
        ps_av_g = [nc.psum_tensor(f"ps_av{i}", [128, 512], F32) for i in range(2)]
        ps_av = [g.__enter__() for g in ps_av_g]

        kv_load, head_te_end = [], []
        exp_done, avev_done, atw_done = {}, {}, {}
        ptev_list = []
        ptg = 0

        av_mm_done = {}
        ptev_head_end = {}

        def emit_av_unit(hh, gq, dv):
            gd = 2 * gq + dv
            hb1 = hh % 2

            def p4_te_av(te, hh=hh, hb1=hb1, gq=gq, dv=dv, gd=gd):
                if gd == 0:
                    te.wait_ge(sV.h, ptev_head_end[hh])
                    if hh >= 2:
                        te.wait_ge(sSC.h, avev_done[(hh - 2, 2)])
                        te.wait_ge(sSC.h, avev_done[(hh - 2, 3)])
                if gd >= 2:
                    te.wait_ge(sSC.h, avev_done[(hh, gd - 2)])
                bank = ps_av[gd % 2]
                first = True
                ins = None
                for r in range(2):
                    for u in range(4 * gq + 4):
                        q0 = max(u, 4 * gq)
                        lhs = vhb[:, hb1 * 4096 + r * 2048 + u * 256 + dv * 128:
                                  hb1 * 4096 + r * 2048 + u * 256 + (dv + 1) * 128]
                        rhs = PT[:, hb1 * 16384 + (r * 8 + u) * 1024 + q0 * 128:
                                 hb1 * 16384 + (r * 8 + u) * 1024 + (4 * gq + 4) * 128]
                        ins = te.matmul(bank[:, (q0 - 4 * gq) * 128:512],
                                        lhs, rhs, start=first, stop=False,
                                        skip_group_check=True)
                        first = False
                sTE.inc(ins)
            TE(p4_te_av)
            av_mm_done[(hh, gd)] = sTE.n
            av_mm = sTE.n

            def p4_sc_av(sc, hh=hh, gd=gd, av_mm=av_mm):
                sc.wait_ge(sTE.h, av_mm)
                key = (hh, gd - 2) if gd >= 2 else ((hh - 1, gd + 2) if hh >= 1 else None)
                if key is not None:
                    sc.wait_ge(sSTG.h, atw_done[key])
                sSC.inc(sc.copy(
                    atb[:, (gd % 2) * 512:(gd % 2 + 1) * 512],
                    ps_av[gd % 2][:, :]))
            SC(p4_sc_av)
            avev_done[(hh, gd)] = sSC.n

            def p4_sy_av(sync, hh=hh, gq=gq, dv=dv, gd=gd):
                sync.wait_ge(sSC.h, avev_done[(hh, gd)])
                c = 2 * hh + dv
                dst = attnT_d[4 * gq * 128:(4 * gq + 4) * 128,
                              c * 128:(c + 1) * 128].rearrange(
                    "(u p) q -> p u q", p=128)
                srcap = atb[:, (gd % 2) * 512:(gd % 2 + 1) * 512].rearrange(
                    "p (u q) -> p u q", u=4)
                sSTG.dinc(sync.dma_start(out=dst, in_=srcap))
            SY(p4_sy_av)
            atw_done[(hh, gd)] = sSTG.n

        for h in range(HEADS):
            hb = h % 2

            def p4_g(g, h=h, hb=hb):
                if h == 0:
                    g.wait_ge(sOUT.h, POLL_DONE)
                    g.wait_ge(sTE.h, Q_TE_END)
                    g.wait_ge(sEVV.h, Q_EV_END)
                    g.wait_ge(sSTG.h, STG_PRE_ARMS)
                if h >= 2:
                    g.wait_ge(sTE.h, av_mm_done[(h - 2, 3)])
                for dh in range(2):
                    src = qloc[(2 * h + dh) * 128:(2 * h + dh + 1) * 128, :]
                    dst = qTh[:, hb * 2048 + dh * 1024:hb * 2048 + (dh + 1) * 1024]
                    sWG.dinc(g.dma_start(out=dst, in_=src))
                for dh in range(2):
                    for s in range(2):
                        src = KSH[s * D + (2 * h + dh) * 128:
                                  s * D + (2 * h + dh + 1) * 128, :]
                        dst = kT[:, hb * 4096 + dh * 2048 + s * 1024:
                                 hb * 4096 + dh * 2048 + (s + 1) * 1024]
                        sWG.dinc(g.dma_start(out=dst, in_=src))
                for s in range(2):
                    src = VSH[s * TOK:(s + 1) * TOK, h * 256:(h + 1) * 256] \
                        .rearrange("(u p) d -> p u d", p=128)
                    dst = vhb[:, hb * 4096 + s * 2048:hb * 4096 + (s + 1) * 2048] \
                        .rearrange("p (u d) -> p u d", u=8)
                    sWG.dinc(g.dma_start(out=dst, in_=src[:, :, :]))
            GE(p4_g)
            kv_load.append(sWG.n)

            for j in range(TC):
                kw = (j + 1) * 128

                def p4_te_s(te, h=h, hb=hb, j=j, kw=kw):
                    if j == 0:
                        te.wait_ge(sWG.h, kv_load[h])
                    if (h, j) != (0, 0):
                        pj, ph = (j - 1, h) if j > 0 else (TC - 1, h - 1)
                        te.wait_ge(sSC.h, exp_done[(ph, pj)])
                    for dh in range(2):
                        lhs = qTh[:, hb * 2048 + dh * 1024 + j * 128:
                                  hb * 2048 + dh * 1024 + (j + 1) * 128]
                        for r in range(2):
                            base = hb * 4096 + dh * 2048 + r * 1024
                            off = 0
                            while off < kw:
                                w = min(512, kw - off)
                                te.matmul(ps_s[:, r * 1024 + off:r * 1024 + off + w],
                                          lhs, kT[:, base + off:base + off + w],
                                          start=(dh == 0), stop=False,
                                          skip_group_check=True)
                                off += w
                    ins = None
                    for r in range(2):
                        ins = te.matmul(
                            ps_s[:, r * 1024 + j * 128:r * 1024 + (j + 1) * 128],
                            ident[:],
                            maskb[:, j * 256 + r * 128:j * 256 + (r + 1) * 128],
                            start=False, stop=(r == 1), skip_group_check=True)
                    sTE.inc(ins)
                TE(p4_te_s)
                s_done = sTE.n

                if h >= 1 and j % 2 == 0:
                    emit_av_unit(h - 1, (j // 2) // 2, (j // 2) % 2)

                def p4_v1(v, kw=kw, s_done=s_done):
                    v.wait_ge(sTE.h, s_done)
                    v.reduce_max(st[:, 8:9], ps_s[:, 0:kw], axis=AX.X)
                    k1 = sV.inc(v.reduce_max(st[:, 9:10], ps_s[:, 1024:1024 + kw], axis=AX.X))
                    v.wait_ge(sV.h, k1)
                    k2 = sV.inc(v.tensor_max(st[:, 10:11], st[:, 8:9], st[:, 9:10]))
                    v.wait_ge(sV.h, k2)
                    sV.inc(v.tensor_scalar_mul(st[:, 11:12], st[:, 10:11], -0.0625))
                VE(p4_v1)
                mneg_ready = sV.n

                def p4_sc1(sc, kw=kw, mneg_ready=mneg_ready):
                    sc.wait_ge(sV.h, mneg_ready)
                    sc.activation(Psb[:, 0:kw], ps_s[:, 0:kw], AF.Exp,
                                  bias=st[:, 11:12], scale=0.0625,
                                  accum_out=st[:, 12:13])
                    sSC.inc(sc.activation(Psb[:, 1024:1024 + kw],
                                          ps_s[:, 1024:1024 + kw], AF.Exp,
                                          bias=st[:, 11:12], scale=0.0625,
                                          accum_out=st[:, 13:14]))
                SC(p4_sc1)
                exp_done[(h, j)] = sSC.n

                def p4_v2(v, h=h, j=j):
                    v.wait_ge(sSC.h, exp_done[(h, j)])
                    k1 = sV.inc(v.tensor_add(st[:, 14:15], st[:, 12:13], st[:, 13:14]))
                    v.wait_ge(sV.h, k1)
                    k2 = sV.inc(v.reciprocal(st[:, 15:16], st[:, 14:15]))
                    v.wait_ge(sV.h, k2)
                    sV.inc(v.tensor_scalar_mul(diag[:], ident[:], st[:, 15:16]))
                VE(p4_v2)
                diag_ready = sV.n

                first_grp = True
                for r in range(2):
                    u = 0
                    while u <= j:
                        nt = min(4, j + 1 - u)
                        bank_i = ptg % 2

                        def p4_te_pt(te, j=j, r=r, u=u, nt=nt, bank_i=bank_i,
                                     first_grp=first_grp, diag_ready=diag_ready):
                            if first_grp:
                                te.wait_ge(sV.h, diag_ready)
                            if len(ptev_list) >= 2:
                                te.wait_ge(sV.h, ptev_list[-2])
                            bank = ps_pt[bank_i]
                            ins = None
                            for q in range(nt):
                                ins = te.matmul(
                                    bank[:, q * 128:(q + 1) * 128],
                                    Psb[:, r * 1024 + (u + q) * 128:
                                        r * 1024 + (u + q + 1) * 128],
                                    diag[:], start=True, stop=True)
                            sTE.inc(ins)
                        TE(p4_te_pt)
                        ptk = sTE.n

                        def p4_v3(v, hb=hb, j=j, r=r, u=u, nt=nt, bank_i=bank_i, ptk=ptk):
                            v.wait_ge(sTE.h, ptk)
                            bank = ps_pt[bank_i]
                            dst = PT[:, hb * 16384:(hb + 1) * 16384].rearrange(
                                "p (t q) -> p t q", t=16)[
                                :, r * 8 + u:r * 8 + u + nt, j * 128:(j + 1) * 128]
                            src = bank[:, 0:nt * 128].rearrange("p (a b) -> p a b", a=nt)
                            sV.inc(v.tensor_copy(dst, src))
                        VE(p4_v3)
                        ptev_list.append(sV.n)
                        ptg += 1
                        first_grp = False
                        u += nt

            ptev_head_end[h] = sV.n
            head_te_end.append(sTE.n)

        for gq in range(2):
            for dv in range(2):
                emit_av_unit(HEADS - 1, gq, dv)

        ATW_END = sSTG.n
        P4_SC_END = sSC.n
        P4_V_END = sV.n
        P4_TE_END = sTE.n

        for g in reversed(ps_av_g):
            g.__exit__(None, None, None)
        for g in reversed(ps_pt_g):
            g.__exit__(None, None, None)
        ps_s_g.__exit__(None, None, None)
        al.free("atb")
        al.free("diag")
        al.free("PT")
        al.free("Psb")
        al.free("vhb")
        al.free("qTh")
        al.free("kT")

        blk2.__exit__(None, None, None)

        blk3 = nc.Block("b3")
        blk3.__enter__()
        TE, VE, SC, GE, SY = blk3.tensor, blk3.vector, blk3.scalar, blk3.gpsimd, blk3.sync

        a2tb = al.get("a2tb", [128, DC * 128], BF16)
        b2b = al.get("b2b", [128, D], BF16)
        wr8b = al.get("wr8b", [8, D], BF16)
        atld = al.get("atld", [128, 2 * D], BF16)
        woblk = al.get("woblk", [128, 2 * DC * 512], BF16)
        ysb = al.get("ysb", [128, 4 * 512], F32)

        def t0_g(g):
            g.wait_ge(sTE.h, P4_TE_END)
            g.wait_ge(sV.h, P4_V_END)
            g.wait_ge(sSC.h, P4_SC_END)
            g.wait_ge(sSTG.h, ATW_END)
            sWG.dinc(g.dma_start(out=a2tb[:, :].rearrange("p (c o) -> p c o", c=DC),
                                 in_=a2t_in.rearrange("(c p) o -> p c o", p=128)[:, :, :]))
            sWG.dinc(g.dma_start(out=b2b[:], in_=b2_in[:, :]))
            sWG.dinc(g.dma_start(out=wr8b[:], in_=wr_in[:, :]))
        GE(t0_g)
        FFN_G = sWG.n
        ps5_g = [nc.psum_tensor(f"ps5_{i}", [128, 512], F32) for i in range(8)]
        ps5 = [g.__enter__() for g in ps5_g]

        wo_load, at_load = [], {}
        wo_mm, wo_ev, y_w = {}, {}, {}

        for ob in range(8):
            def p5_g(g, ob=ob):
                if ob == 0:
                    g.wait_ge(sTE.h, P4_TE_END)
                    g.wait_ge(sV.h, P4_V_END)
                    g.wait_ge(sSC.h, P4_SC_END)
                if ob >= 2:
                    g.wait_ge(sTE.h, wo_mm[(ob - 2, TC - 1)])
                src = w_in["wo"][:, ob * 512:(ob + 1) * 512].rearrange(
                    "(c p) o -> p c o", p=128)
                dst = woblk[:, (ob % 2) * DC * 512:(ob % 2 + 1) * DC * 512].rearrange(
                    "p (c o) -> p c o", c=DC)
                sWG.dinc(g.dma_start(out=dst, in_=src[:, :, :]))
            GE(p5_g)
            wo_load.append(sWG.n)

            for tk in range(TC):
                def p5_at(sync, ob=ob, tk=tk):
                    if ob == 0 and tk == 0:
                        sync.wait_ge(sSTG.h, ATW_END)
                        sync.wait_ge(sTE.h, P4_TE_END)
                    prev = (ob, tk - 2) if tk >= 2 else ((ob - 1, tk + 6) if ob >= 1 else None)
                    if prev is not None:
                        sync.wait_ge(sTE.h, wo_mm[prev])
                    sXQ.dinc(sync.dma_start(
                        out=atld[:, (tk % 2) * D:(tk % 2 + 1) * D],
                        in_=attnT_d[tk * 128:(tk + 1) * 128, :]))
                SY(p5_at)
                at_load[(ob, tk)] = sXQ.n

                def p5_te(te, ob=ob, tk=tk):
                    te.wait_ge(sWG.h, wo_load[ob])
                    te.wait_ge(sXQ.h, at_load[(ob, tk)])
                    if ob >= 1:
                        te.wait_ge(sEVV.h if tk % 2 == 0 else sEVS.h,
                                   wo_ev[(ob - 1, tk)])
                    bank = ps5[tk]
                    ins = None
                    for c in range(DC):
                        lhs = atld[:, (tk % 2) * D + c * 128:(tk % 2) * D + (c + 1) * 128]
                        rhs = woblk[:, (ob % 2) * DC * 512 + c * 512:
                                    (ob % 2) * DC * 512 + (c + 1) * 512]
                        ins = te.matmul(bank[:, :], lhs, rhs,
                                        start=(c == 0), stop=(c == DC - 1))
                    sTE.inc(ins)
                TE(p5_te)
                wo_mm[(ob, tk)] = sTE.n

                nsl = (ob * TC + tk) % 4

                def p5_ev(eng, ob=ob, tk=tk, nsl=nsl):
                    eng.wait_ge(sTE.h, wo_mm[(ob, tk)])
                    if ob * TC + tk >= 4:
                        eng.wait_ge(sSTG.h, y_w[ob * TC + tk - 4])
                    sem = sEVV if tk % 2 == 0 else sEVS
                    cp = eng.tensor_copy if tk % 2 == 0 else eng.copy
                    sem.inc(cp(ysb[:, nsl * 512:(nsl + 1) * 512], ps5[tk][:, :]))
                (VE if tk % 2 == 0 else SC)(p5_ev)
                wo_ev[(ob, tk)] = (sEVV if tk % 2 == 0 else sEVS).n

                def p5_yw(sync, ob=ob, tk=tk, nsl=nsl):
                    sync.wait_ge(sEVV.h if tk % 2 == 0 else sEVS.h, wo_ev[(ob, tk)])
                    sSTG.dinc(sync.dma_start(
                        out=y_d[tk * 128:(tk + 1) * 128, ob * 512:(ob + 1) * 512],
                        in_=ysb[:, nsl * 512:(nsl + 1) * 512]))
                SY(p5_yw)
                y_w[ob * TC + tk] = sSTG.n

        P5_TE_END = sTE.n

        for g in reversed(ps5_g):
            g.__exit__(None, None, None)
        al.free("ysb")
        al.free("woblk")
        al.free("atld")

        ybuf = al.get("ybuf", [128, 2 * D], F32)
        x1sb = al.get("x1sb", [128, 2 * D], F32)
        ln2c = al.get("ln2c", [128, D], BF16)
        sqd = al.get("sqd", [128, D], BF16)
        l2t = al.get("l2t", [128, D], BF16)
        fsb = al.get("fsb", [128, D], F32)
        outb = al.get("outb", [128, D], F32)
        lowt = al.get("lowt", [128, 128], BF16)
        rrb = al.get("rrb", [8, 128], BF16)
        rtf = al.get("rtf", [128, 8], F32)
        ex8 = al.get("ex8", [128, 8], F32)
        scb = al.get("scb", [128, 8], BF16)
        scTb = al.get("scTb", [8, 128], BF16)
        lowS = al.get("lowS", [128, 128], BF16)
        psT_g = [nc.psum_tensor(f"psT_{i}", [128, 512], F32) for i in range(8)]
        psT = [g.__enter__() for g in psT_g]

        P5_EVV_END = sEVV.n
        P5_EVS_END = sEVS.n

        wrt_ev2 = []
        for c in range(DC):
            def w_te(te, c=c):
                if c == 0:
                    te.wait_ge(sWG.h, FFN_G)
                    te.wait_ge(sEVV.h, P5_EVV_END)
                    te.wait_ge(sEVS.h, P5_EVS_END)
                else:
                    te.wait_ge(sEVV.h, wrt_ev2[-1])
                sTE.inc(te.matmul(psT[2][:, 0:8], wr8b[0:8, c * 128:(c + 1) * 128],
                                  i8b[:, :], start=True, stop=True))
            TE(w_te)
            wtp = sTE.n

            def w_v(v, c=c, wtp=wtp):
                v.wait_ge(sTE.h, wtp)
                sEVV.inc(v.tensor_copy(wrtb[:, c * 8:(c + 1) * 8], psT[2][:, 0:8]))
            VE(w_v)
            wrt_ev2.append(sEVV.n)
        WRTB_EV_END = sEVV.n

        norm1_hist, x1add_hist, outadd_hist, out_w = [], [], [], []
        prev_l2t_ev = None
        prev_tp_end = None
        b2_free = []
        ffn_ev = {}

        for tk in range(TC):
            sl = tk % 2

            def t_sy1(sync, tk=tk, sl=sl):
                if tk == 0:
                    sync.wait_ge(sTE.h, P5_TE_END)
                    sync.wait_ge(sEVV.h, P5_EVV_END)
                    sync.wait_ge(sEVS.h, P5_EVS_END)
                if tk >= 2:
                    sync.wait_ge(sV.h, norm1_hist[tk - 2])
                sync.wait_ge(sSTG.h, y_w[7 * TC + tk])
                sXQ.dinc(sync.dma_start(out=ybuf[:, sl * D:(sl + 1) * D],
                                        in_=y_d[tk * 128:(tk + 1) * 128, :]))
            SY(t_sy1)
            yx_load = sXQ.n

            ys = ybuf[:, sl * D:(sl + 1) * D]
            x1s = x1sb[:, sl * D:(sl + 1) * D]
            l2s = ln2c[:, :]
            os_ = outb[:, :]

            def t_sc0(sc, ys=ys, yx_load=yx_load):
                sc.wait_ge(sXQ.h, yx_load)
                sSC.inc(sc.activation(sqd[:], ys, AF.Square, accum_out=st[:, 1:2]))
            SC(t_sc0)
            sumsq1 = sSC.n

            def t_v1(v, tk=tk, ys=ys, yx_load=yx_load):
                v.wait_ge(sXQ.h, yx_load)
                kacc = sV.inc(v.tensor_scalar(fsb[:], ys, 1.0, 0.0, ALU.mult,
                                              ALU.add, accum_out=st[:, 0:1]))
                v.wait_ge(sV.h, kacc)
                v.wait_ge(sSC.h, sumsq1)
                v.tensor_scalar_mul(st[:, 2:3], st[:, 0:1], 1.0 / D)
                v.tensor_scalar_mul(st[:, 5:6], st[:, 1:2], 1.0 / D)
                kmu = sV.inc(v.tensor_mul(st[:, 6:7], st[:, 2:3], st[:, 2:3]))
                v.wait_ge(sV.h, kmu)
                sV.inc(v.tensor_sub(st[:, 3:4], st[:, 5:6], st[:, 6:7]))
            VE(t_v1)
            var1 = sV.n

            def t_sc1(sc, var1=var1):
                sc.wait_ge(sV.h, var1)
                sSC.inc(sc.activation(st[:, 7:8], st[:, 3:4], AF.Sqrt, bias=st[:, 30:31]))
            SC(t_sc1)
            rstd1 = sSC.n

            def t_v2(v, tk=tk, ys=ys, x1s=x1s, rstd1=rstd1):
                v.wait_ge(sSC.h, rstd1)
                if tk >= 2:
                    v.wait_ge(sG.h, outadd_hist[tk - 2])
                kr = sV.inc(v.reciprocal(st[:, 4:5], st[:, 7:8]))
                v.wait_ge(sV.h, kr)
                sV.inc(v.tensor_scalar(x1s, ys, st[:, 2:3], st[:, 4:5],
                                       ALU.subtract, ALU.mult))
            VE(t_v2)
            norm1_hist.append(sV.n)

            def t_g1(g, tk=tk, x1s=x1s, n1=norm1_hist[-1]):
                g.wait_ge(sV.h, n1)
                sWG.dinc(g.dma_start(out=x1s[:, 0:D // 2],
                                     in_=x_in[tk * 128:(tk + 1) * 128, 0:D // 2],
                                     accum_op=ALU.add))
                sWG.dinc(g.dma_start(out=x1s[:, D // 2:D],
                                     in_=x_in[tk * 128:(tk + 1) * 128, D // 2:D],
                                     accum_op=ALU.add))
            GE(t_g1)
            x1add_hist.append(sWG.n)
            x1_done = sWG.n

            def t_sc15(sc, x1s=x1s, x1_done=x1_done):
                sc.wait_ge(sWG.h, x1_done)
                sSC.inc(sc.activation(sqd[:], x1s, AF.Square, accum_out=st[:, 9:10]))
            SC(t_sc15)
            sumsq2 = sSC.n

            def t_v3(v, x1s=x1s, x1_done=x1_done):
                v.wait_ge(sWG.h, x1_done)
                kacc = sV.inc(v.tensor_scalar(fsb[:], x1s, 1.0, 0.0, ALU.mult,
                                              ALU.add, accum_out=st[:, 8:9]))
                v.wait_ge(sV.h, kacc)
                v.wait_ge(sSC.h, sumsq2)
                v.tensor_scalar_mul(st[:, 10:11], st[:, 8:9], 1.0 / D)
                v.tensor_scalar_mul(st[:, 12:13], st[:, 9:10], 1.0 / D)
                kmu = sV.inc(v.tensor_mul(st[:, 13:14], st[:, 10:11], st[:, 10:11]))
                v.wait_ge(sV.h, kmu)
                sV.inc(v.tensor_sub(st[:, 11:12], st[:, 12:13], st[:, 13:14]))
            VE(t_v3)
            var2 = sV.n

            def t_sc2(sc, var2=var2):
                sc.wait_ge(sV.h, var2)
                sSC.inc(sc.activation(st[:, 14:15], st[:, 11:12], AF.Sqrt, bias=st[:, 30:31]))
            SC(t_sc2)
            rstd2 = sSC.n

            def t_v4(v, tk=tk, x1s=x1s, l2s=l2s, rstd2=rstd2):
                v.wait_ge(sSC.h, rstd2)
                if prev_tp_end is not None:
                    v.wait_ge(sTE.h, prev_tp_end)
                kr = sV.inc(v.reciprocal(st[:, 15:16], st[:, 14:15]))
                v.wait_ge(sV.h, kr)
                sV.inc(v.tensor_scalar(l2s, x1s, st[:, 10:11], st[:, 15:16],
                                       ALU.subtract, ALU.mult))
            VE(t_v4)
            ln2_done = sV.n

            l2t_ev = []
            for gi in range(8):
                def t_te1(te, tk=tk, gi=gi, l2s=l2s, ln2_done=ln2_done,
                          prev_ev=prev_l2t_ev):
                    if tk == 0 and gi == 0:
                        te.wait_ge(sEVV.h, P5_EVV_END)
                        te.wait_ge(sEVS.h, P5_EVS_END)
                    if gi == 0:
                        te.wait_ge(sV.h, ln2_done)
                    if len(l2t_ev) >= 2:
                        te.wait_ge(sEVS.h, l2t_ev[-2])
                    elif prev_ev is not None:
                        te.wait_ge(sEVS.h, prev_ev[gi - 2])
                    bank = psT[gi % 2]
                    ins = None
                    for u in range(4):
                        c = 4 * gi + u
                        ins = te.matmul(bank[:, u * 128:(u + 1) * 128],
                                        l2s[:, c * 128:(c + 1) * 128], ident[:],
                                        start=True, stop=True)
                    sTE.inc(ins)
                TE(t_te1)
                tpd = sTE.n

                def t_sc3(sc, gi=gi, tpd=tpd):
                    sc.wait_ge(sTE.h, tpd)
                    bank = psT[gi % 2]
                    dst = l2t[:, :].rearrange("p (c t) -> p c t", c=DC)[
                        :, 4 * gi:4 * gi + 4, 0:128]
                    src = bank[:, :].rearrange("p (a b) -> p a b", a=4)
                    sEVS.inc(sc.copy(dst, src))
                SC(t_sc3)
                l2t_ev.append(sEVS.n)
            prev_l2t_ev = l2t_ev
            prev_tp_end = sTE.n

            def t_te2(te, tk=tk, lev=l2t_ev[-1]):
                te.wait_ge(sEVS.h, lev)
                if tk == 0:
                    te.wait_ge(sEVV.h, WRTB_EV_END)
                if b2_free:
                    te.wait_ge(sV.h, b2_free[-1])
                ins = None
                for c in range(DC):
                    ins = te.matmul(psT[2][:, 0:128], a2tb[:, c * 128:(c + 1) * 128],
                                    l2t[:, c * 128:(c + 1) * 128],
                                    start=(c == 0), stop=(c == DC - 1),
                                    skip_group_check=True)
                for c in range(DC):
                    ins = te.matmul(psT[2][0:8, 128:256], wrtb[:, c * 8:(c + 1) * 8],
                                    l2t[:, c * 128:(c + 1) * 128],
                                    start=(c == 0), stop=False,
                                    skip_group_check=True)
                ins = te.matmul(psT[2][0:8, 128:256], brb[:, :], ones1[:, :],
                                start=False, stop=True, skip_group_check=True)
                sTE.inc(ins)
            TE(t_te2)
            rl_mm = sTE.n

            def t_v5(v, rl_mm=rl_mm):
                v.wait_ge(sTE.h, rl_mm)
                v.tensor_copy(lowt[:], psT[2][:, 0:128])
                sV.inc(v.tensor_copy(rrb[:], psT[2][0:8, 128:256]))
            VE(t_v5)
            b2_free.append(sV.n)

            def t_te3(te, b2f=b2_free[-1]):
                te.wait_ge(sV.h, b2f)
                sTE.inc(te.matmul(psT[3][:, 0:8], rrb[:], i8b[:, :],
                                  start=True, stop=True))
            TE(t_te3)
            rt_mm = sTE.n

            def t_v6(v, rt_mm=rt_mm):
                v.wait_ge(sTE.h, rt_mm)
                k1 = sV.inc(v.tensor_copy(rtf[:], psT[3][:, 0:8]))
                v.wait_ge(sV.h, k1)
                k2 = sV.inc(v.reduce_max(st[:, 24:25], rtf[:], axis=AX.X))
                v.wait_ge(sV.h, k2)
                sV.inc(v.tensor_scalar_mul(st[:, 25:26], st[:, 24:25], -1.0))
            VE(t_v6)
            rsm = sV.n

            def t_sc4(sc, rsm=rsm):
                sc.wait_ge(sV.h, rsm)
                sSC.inc(sc.activation(ex8[:], rtf[:], AF.Exp, bias=st[:, 25:26],
                                      accum_out=st[:, 26:27]))
            SC(t_sc4)
            exp8_done = sSC.n

            def t_v7(v, exp8_done=exp8_done):
                v.wait_ge(sSC.h, exp8_done)
                k1 = sV.inc(v.reciprocal(st[:, 27:28], st[:, 26:27]))
                v.wait_ge(sV.h, k1)
                sV.inc(v.tensor_scalar(scb[:], ex8[:], st[:, 27:28], 16.0,
                                       ALU.mult, ALU.mult))
            VE(t_v7)
            scb_done = sV.n

            def t_te4(te, scb_done=scb_done):
                te.wait_ge(sV.h, scb_done)
                sTE.inc(te.matmul(psT[3][0:8, 128:256], scb[:], ident[:],
                                  start=True, stop=True))
            TE(t_te4)
            sct_mm = sTE.n

            def t_v8(v, sct_mm=sct_mm):
                v.wait_ge(sTE.h, sct_mm)
                sV.inc(v.tensor_copy(scTb[:], psT[3][0:8, 128:256]))
            VE(t_v8)
            sct_done = sV.n

            def t_te5(te, sct_done=sct_done):
                te.wait_ge(sV.h, sct_done)
                sTE.inc(te.matmul(psT[3][:, 256:384], e8b[:], scTb[:],
                                  start=True, stop=True))
            TE(t_te5)
            ex_mm = sTE.n

            def t_v9(v, ex_mm=ex_mm):
                v.wait_ge(sTE.h, ex_mm)
                sV.inc(v.tensor_tensor(lowS[:], lowt[:], psT[3][:, 256:384], ALU.mult))
            VE(t_v9)
            lowS_done = sV.n

            for half in range(2):
                def t_te6(te, tk=tk, half=half, lowS_done=lowS_done):
                    if half == 0:
                        te.wait_ge(sV.h, lowS_done)
                    if (tk, half) != (0, 0):
                        pk = (tk, half - 1) if half == 1 else (tk - 1, 1)
                        for eng_sem, val in ffn_ev[pk]:
                            te.wait_ge(eng_sem.h, val)
                    ins = None
                    for og4 in range(4):
                        og = half * 4 + og4
                        ins = te.matmul(psT[4 + og4][:, :], lowS[:],
                                        b2b[:, og * 512:(og + 1) * 512],
                                        start=True, stop=True)
                    sTE.inc(ins)
                TE(t_te6)
                f_mm = sTE.n

                evs = []

                def t_vf(v, half=half, f_mm=f_mm):
                    v.wait_ge(sTE.h, f_mm)
                    v.tensor_copy(fsb[:, (half * 4) * 512:(half * 4 + 1) * 512],
                                  psT[4][:, :])
                    sEVV.inc(v.tensor_copy(
                        fsb[:, (half * 4 + 1) * 512:(half * 4 + 2) * 512],
                        psT[5][:, :]))
                VE(t_vf)
                evs.append((sEVV, sEVV.n))

                def t_sf(sc, half=half, f_mm=f_mm):
                    sc.wait_ge(sTE.h, f_mm)
                    sc.copy(fsb[:, (half * 4 + 2) * 512:(half * 4 + 3) * 512],
                            psT[6][:, :])
                    sEVS.inc(sc.copy(
                        fsb[:, (half * 4 + 3) * 512:(half * 4 + 4) * 512],
                        psT[7][:, :]))
                SC(t_sf)
                evs.append((sEVS, sEVS.n))
                ffn_ev[(tk, half)] = evs

            def t_sc45(sc, tk=tk):
                for eng_sem, val in ffn_ev[(tk, 0)] + ffn_ev[(tk, 1)]:
                    sc.wait_ge(eng_sem.h, val)
                sSC.inc(sc.activation(sqd[:], fsb[:], AF.Square, accum_out=st[:, 17:18]))
            SC(t_sc45)
            sumsq3 = sSC.n

            def t_v10(v, tk=tk, os_=os_):
                for eng_sem, val in ffn_ev[(tk, 0)] + ffn_ev[(tk, 1)]:
                    v.wait_ge(eng_sem.h, val)
                if tk >= 1:
                    v.wait_ge(sOUT.h, out_w[tk - 1])
                kacc = sV.inc(v.tensor_scalar(os_, fsb[:], 1.0, 0.0, ALU.mult,
                                              ALU.add, accum_out=st[:, 16:17]))
                v.wait_ge(sV.h, kacc)
                v.wait_ge(sSC.h, sumsq3)
                v.tensor_scalar_mul(st[:, 18:19], st[:, 16:17], 1.0 / D)
                v.tensor_scalar_mul(st[:, 21:22], st[:, 17:18], 1.0 / D)
                kmu = sV.inc(v.tensor_mul(st[:, 20:21], st[:, 18:19], st[:, 18:19]))
                v.wait_ge(sV.h, kmu)
                sV.inc(v.tensor_sub(st[:, 19:20], st[:, 21:22], st[:, 20:21]))
            VE(t_v10)
            var3 = sV.n

            def t_sc5(sc, var3=var3):
                sc.wait_ge(sV.h, var3)
                sSC.inc(sc.activation(st[:, 22:23], st[:, 19:20], AF.Sqrt, bias=st[:, 30:31]))
            SC(t_sc5)
            rstd3 = sSC.n

            def t_v11(v, os_=os_, rstd3=rstd3):
                v.wait_ge(sSC.h, rstd3)
                kr = sV.inc(v.reciprocal(st[:, 23:24], st[:, 22:23]))
                v.wait_ge(sV.h, kr)
                sV.inc(v.tensor_scalar(os_, fsb[:], st[:, 18:19], st[:, 23:24],
                                       ALU.subtract, ALU.mult))
            VE(t_v11)
            norm3 = sV.n

            def t_g2(g, os_=os_, x1s=x1s, norm3=norm3):
                g.wait_ge(sV.h, norm3)
                sG.inc(g.tensor_add(os_, os_, x1s))
            GE(t_g2)
            outadd_hist.append(sG.n)
            oadd = sG.n

            def t_syo(sync, tk=tk, os_=os_, oadd=oadd):
                sync.wait_ge(sG.h, oadd)
                sOUT.dinc(sync.dma_start(out=out_hbm[tk * 128:(tk + 1) * 128, :],
                                         in_=os_))
            SY(t_syo)
            out_w.append(sOUT.n)

        OUT_END = sOUT.n

        for g in reversed(psT_g):
            g.__exit__(None, None, None)

        def fin_wait(sync):
            sync.wait_ge(sOUT.h, OUT_END)
        SY(fin_wait)

        blk3.__exit__(None, None, None)

        FIN_PRE = sSTG.n
        for arm in nc.switch_core_id([nc.sync], n=128):
            slot = int(arm) % 2
            nc.sync.dma_start(out=SHF[0:1, 2 + slot:3 + slot],
                              in_=mg[0:1, 1:2]).then_inc(sSTG.h, 16)
            nc.sync.wait_ge(sSTG.h, FIN_PRE + 16)
        sSTG.n = FIN_PRE + 16

        blk4 = nc.Block("b4")
        blk4.__enter__()

        def p_final(sync):
            emit_poll(sync, 2, 1, sPOLL2)
            if DEBUG:
                sOUT.dinc(sync.dma_start(out=dbg["k"][:, :], in_=kloc[:, :]))
                sOUT.dinc(sync.dma_start(out=dbg["v"][:, :], in_=vloc[:, :]))
                sOUT.dinc(sync.dma_start(out=dbg["q"][:, :], in_=qloc[:, :]))
                sOUT.dinc(sync.dma_start(out=dbg["at"][:, :], in_=attnT_d[:, :]))
                sOUT.dinc(sync.dma_start(out=dbg["y"][:, :], in_=y_d[:, :]))
                sync.wait_ge(sOUT.h, sOUT.n)
            sync.wait_ge(sOUT.h, OUT_END)
        blk4.sync(p_final)
        blk4.__exit__(None, None, None)

        for side in ("left", "right"):
            while al.stacks[side]:
                _, g = al.stacks[side].pop()
                g.__exit__(None, None, None)

    return nc


_CACHE = {}


def kernel(**inputs):
    import jax
    epoch = next(_EPOCH)
    ready_magic = np.int32(0x5AFE0000 + 2 * epoch)
    fin_magic = np.int32(0x5AFE0000 + 2 * epoch + 1)

    devs = jax.devices()[:8]
    hw = [int(getattr(d, "local_hardware_id", i)) for i, d in enumerate(devs)]

    X = np.asarray(inputs["L0_attn_hidden_states"], dtype=np.float32)
    wq = np.asarray(inputs["L0_attn_Wq"], dtype=np.float32)
    wk = np.asarray(inputs["L0_attn_Wk"], dtype=np.float32)
    wv = np.asarray(inputs["L0_attn_Wv"], dtype=np.float32)
    wo = np.asarray(inputs["L0_attn_Wo"], dtype=np.float32)
    A = np.asarray(inputs["L0_ffn_A"], dtype=np.float32)
    Bm = np.asarray(inputs["L0_ffn_B"], dtype=np.float32)
    Wr = np.asarray(inputs["L0_ffn_Wr"], dtype=np.float32)
    br = np.asarray(inputs["L0_ffn_br"], dtype=np.float32)

    a2t = np.ascontiguousarray(A.reshape(128, D).T)
    b2 = np.ascontiguousarray(Bm.transpose(0, 2, 1).reshape(128, D))
    wr8 = np.ascontiguousarray(Wr)
    br8 = np.ascontiguousarray(br.reshape(1, 8))
    magic = np.array([[ready_magic, fin_magic, 0, 0]], np.int32)

    in_maps = []
    for i in range(8):
        p, b = hw[i] % 2, hw[i] // 2
        cos_fm, sin_fm = _rope_tables(p)
        in_maps.append({
            "x_own": np.ascontiguousarray(
                X[b].reshape(16, 128, D)[p::2].reshape(TOK, D)),
            "wq": wq, "wk": wk, "wv": wv, "wo": wo,
            "a2t": a2t, "b2": b2, "wr8": wr8, "br8": br8,
            "cosfm_i": cos_fm, "sinfm_i": sin_fm,
            "maskq": _mask_tables(p),
            "magic": magic,
        })

    nc = build_program()
    res = run_bass_kernel_spmd(nc, in_maps, core_ids=list(range(8)))

    OUT = np.empty((4, 2048, D), np.float32)
    for i in range(8):
        p, b = hw[i] % 2, hw[i] // 2
        OUT[b].reshape(16, 128, D)[p::2] = res.results[i]["out"].reshape(TC, 128, D)
    return OUT
